# version 11
# baseline (speedup 1.0000x reference)
import contextlib
import numpy as np
import concourse.bass as bass
import concourse.mybir as mybir
from concourse.bass_utils import run_bass_kernel_spmd

F32 = mybir.dt.float32
I32 = mybir.dt.int32
BF16 = mybir.dt.bfloat16
AF = mybir.ActivationFunctionType
ALU = mybir.AluOpType
AX = mybir.AxisListType

P, S, D, DH, HDS, DK, LAT = 128, 2048, 1024, 4096, 16, 64, 512
TT, ST, KC, NPAIR = S // P, DH // P, D // P, HDS // 2

RB = -0.0009765625
FFN_STAGE = 3
A1_LEVEL = 3
N_SUBLAYERS = 6

MEAN, VAR, RSTD, XMAX, XS, ALPHA, BETA, SC2 = 0, 16, 32, 48, 64, 80, 96, 112
NW1 = 144
THR1, THR2, NTH1, NTH2, M1C, M2C, NEGM = 192, 193, 194, 195, 196, 197, 198
QA, KA, CA, CB = 200, 208, 216, 217
SC1, SC2_, SC3, SC4 = 220, 221, 222, 223
SCR = 256


class TK:
    def __init__(self, nc):
        self.sem, self.cnt, self.lastw = {}, {}, {}
        for n in ("te", "ve", "se", "ge", "dls", "dss", "dlg", "dsg"):
            self.sem[n] = nc.alloc_semaphore(name="sm_" + n)
            self.cnt[n] = 0
        self.maxwait = {n: 0 for n in self.sem}

    def inc(self, instr, n, amount=1):
        instr.then_inc(self.sem[n], amount)
        self.cnt[n] += amount
        return self.cnt[n]

    def wait(self, eng, ename, n, val):
        if val <= 0:
            return
        self.maxwait[n] = max(self.maxwait[n], val)
        if self.lastw.get((ename, n), -1) >= val:
            return
        eng.wait_ge(self.sem[n], val)
        self.lastw[(ename, n)] = val

    def dma(self, eng, ename, qname, out, in_):
        self.wait(eng, ename, qname, self.cnt[qname])
        return self.inc(eng.dma_start(out=out, in_=in_), qname, 16)

    def sw(self, eng, ename):
        self.wait(eng, ename, ename, self.cnt[ename])

    def snap(self):
        return dict(self.cnt)

    def wait_all(self, eng, ename, snap):
        for n, v in snap.items():
            self.wait(eng, ename, n, v)

    def check(self):
        for n, v in self.maxwait.items():
            assert v <= self.cnt[n], f"wait beyond production: {n} {v} > {self.cnt[n]}"


def build_program():
    nc = bass.Bass("TRN2", target_bir_lowering=False)

    x_h = nc.dram_tensor("x", [S, D], F32, kind="ExternalInput").ap()
    out_h = nc.dram_tensor("out", [S, D], F32, kind="ExternalOutput").ap()
    w = {}
    for L in (0, 1):
        w[f"w1t{L}"] = nc.dram_tensor(f"w1t{L}", [D, DH], F32, kind="ExternalInput").ap()
        w[f"w2t{L}"] = nc.dram_tensor(f"w2t{L}", [DH, D], F32, kind="ExternalInput").ap()
        w[f"cw{L}"] = nc.dram_tensor(f"cw{L}", [P, ST * 3], F32, kind="ExternalInput").ap()
        w[f"cb{L}"] = nc.dram_tensor(f"cb{L}", [P, ST], F32, kind="ExternalInput").ap()
    for nm in ("wq", "wk", "wv"):
        w[nm] = nc.dram_tensor(nm, [NPAIR, P, KC * P], F32, kind="ExternalInput").ap()
    for nm in ("bq", "bk", "bv"):
        w[nm] = nc.dram_tensor(nm, [P, NPAIR], F32, kind="ExternalInput").ap()
    w["wo"] = nc.dram_tensor("wo", [D, D], F32, kind="ExternalInput").ap()
    w["bo"] = nc.dram_tensor("bo", [1, D], F32, kind="ExternalInput").ap()
    w["wcdown"] = nc.dram_tensor("wcdown", [D, LAT], F32, kind="ExternalInput").ap()
    w["wqup"] = nc.dram_tensor("wqup", [LAT, D], F32, kind="ExternalInput").ap()
    w["wkvup"] = nc.dram_tensor("wkvup", [LAT, 2 * DK], F32, kind="ExternalInput").ap()
    w["wcproj"] = nc.dram_tensor("wcproj", [D, D], F32, kind="ExternalInput").ap()

    g_wq = nc.dram_tensor("g_wq", [2, P, 32768], BF16, kind="Internal").ap()
    g_silu = nc.dram_tensor("g_silu", [ST, P, S], F32, kind="Internal").ap()
    g_q = nc.dram_tensor("g_q", [NPAIR, P, S], F32, kind="Internal").ap()
    g_k = nc.dram_tensor("g_k", [NPAIR, P, S], F32, kind="Internal").ap()
    g_v = nc.dram_tensor("g_v", [HDS, P, TT * 65], F32, kind="Internal").ap()

    xres = nc.alloc_sbuf_tensor("xres", [P, TT * D], F32)
    stats = nc.alloc_sbuf_tensor("stats", [P, 352], F32)
    ident_bf = nc.alloc_sbuf_tensor("ident_bf", [P, P], BF16)
    ident_f = nc.alloc_sbuf_tensor("ident_f", [P, P], F32)
    maskt = nc.alloc_sbuf_tensor("maskt", [P, P], F32)
    ones_bf = nc.alloc_sbuf_tensor("ones_bf", [P, 1], BF16)
    ones_f = nc.alloc_sbuf_tensor("ones_f", [P, 1], F32)
    onesrow_f = nc.alloc_sbuf_tensor("onesrow_f", [1, P], F32)
    neg128row = nc.alloc_sbuf_tensor("neg128row", [1, P], F32)
    row2_f = nc.alloc_sbuf_tensor("row2_f", [1, D], F32)
    borow = nc.alloc_sbuf_tensor("borow", [1, D], F32)
    convw_sb = nc.alloc_sbuf_tensor("convw_sb", [P, ST * 3], F32)
    convb_sb = nc.alloc_sbuf_tensor("convb_sb", [P, ST], F32)

    es = contextlib.ExitStack()
    blk = es.enter_context(nc.Block())
    T = TK(nc)

    def consts_and_x():
        def g(ge):
            T.inc(ge.iota(ident_f[:], pattern=[[1, P]], base=0, channel_multiplier=-1,
                          allow_small_or_imprecise_dtypes=True), "ge")
        blk.gpsimd(g)

        def v(ve):
            ve.memset(ones_bf[:], 1.0)
            ve.memset(ones_f[:], 1.0)
            ve.memset(onesrow_f[:], 1.0)
            ve.memset(neg128row[:], -128.0)
            ve.memset(stats[0:64, CA:CA + 1], 1.0)
            ve.memset(stats[64:P, CA:CA + 1], 0.0)
            ve.memset(stats[0:64, CB:CB + 1], 0.0)
            ve.memset(stats[64:P, CB:CB + 1], 1.0)
            T.wait(ve, "ve", "ge", T.cnt["ge"])
            T.sw(ve, "ve")
            T.inc(ve.tensor_single_scalar(ident_bf[:], ident_f[:], 0.0, ALU.is_equal), "ve")
            T.sw(ve, "ve")
            T.inc(ve.tensor_single_scalar(maskt[:], ident_f[:], 0.0, ALU.is_ge), "ve")
            T.sw(ve, "ve")
            T.inc(ve.tensor_single_scalar(ident_f[:], ident_f[:], 0.0, ALU.is_equal), "ve")
        blk.vector(v)

        def sy(sy_):
            for i in range(TT):
                T.dma(sy_, "sy", "dls", out=xres[:, i * D:(i + 1) * D],
                                    in_=x_h[i * P:(i + 1) * P, :])
        blk.sync(sy)

    def ffn_prep(L, ces):
        snap0 = T.snap()
        wbA = ces.enter_context(nc.sbuf_tensor(f"wbA{L}", [P, DH], F32))
        wbB = ces.enter_context(nc.sbuf_tensor(f"wbB{L}", [P, DH], F32))
        wbC = ces.enter_context(nc.sbuf_tensor(f"wbC{L}", [P, DH], F32))
        wng = ces.enter_context(nc.sbuf_tensor(f"wng{L}", [P, DH], F32))
        wsl = ces.enter_context(nc.sbuf_tensor(f"wslP{L}", [P, 32768], BF16))
        pp = ces.enter_context(nc.psum_tensor(f"ppP{L}", [P, 512], F32))
        ppr = ces.enter_context(nc.psum_tensor(f"pprP{L}", [1, D], F32))
        bufs = [wbA, wbB, wbC]
        NTIL = KC + ST

        vred_tick = []
        ld1_tick = {}

        def syl_one(eng, ename, qname, i):
            if i < 2:
                T.wait_all(eng, ename, snap0)
            if i >= 3:
                T.wait(eng, ename, "ve", vred_tick[i - 3])
            src = (w[f"w1t{L}"][i * P:(i + 1) * P, :] if i < KC
                   else w[f"w2t{L}"][(i - KC) * P:(i - KC + 1) * P, :])
            dst = bufs[i % 3][:] if i < KC else bufs[i % 3][:, 0:D]
            ld1_tick[i] = T.dma(eng, ename, qname, out=dst, in_=src)
            if i == NTIL - 1:
                T.dma(eng, ename, qname, out=convw_sb[:], in_=w[f"cw{L}"])
                T.dma(eng, ename, qname, out=convb_sb[:], in_=w[f"cb{L}"])

        def vred_chunk(ve, lo, hi):
            if lo == 0:
                T.wait_all(ve, "ve", snap0)
            for i in range(lo, hi):
                T.wait(ve, "ve", "dls" if i % 2 == 0 else "dlg", ld1_tick[i])
                src = bufs[i % 3][:] if i < KC else bufs[i % 3][:, 0:D]
                T.sw(ve, "ve")
                vred_tick.append(
                    T.inc(ve.tensor_reduce(stats[:, SCR + i:SCR + i + 1], src, AX.X,
                                           ALU.add, apply_absolute_value=True), "ve"))

        for lo in range(0, NTIL, 2):
            hi = min(lo + 2, NTIL)
            blk.sync(lambda sy_, lo=lo: syl_one(sy_, "sy", "dls", lo))
            if lo + 1 < NTIL:
                blk.gpsimd(lambda ge, lo=lo: syl_one(ge, "ge", "dlg", lo + 1))
            blk.vector(lambda ve, lo=lo, hi=hi: vred_chunk(ve, lo, hi))

        def tsum(te):
            T.wait_all(te, "te", snap0)
            T.wait(te, "te", "ve", T.cnt["ve"])
            te.matmul(pp[0:KC, 0:1], stats[:, SCR:SCR + KC], ones_f[:, 0:1],
                      start=True, stop=True)
            T.inc(te.matmul(pp[0:ST, 2:3], stats[:, SCR + KC:SCR + NTIL],
                            ones_f[:, 0:1], start=True, stop=True), "te")
        blk.tensor(tsum)

        def vcp(ve):
            T.wait(ve, "ve", "te", T.cnt["te"])
            T.sw(ve, "ve")
            T.inc(ve.tensor_copy(stats[0:KC, SCR + 44:SCR + 45], pp[0:KC, 0:1]), "ve")
            T.sw(ve, "ve")
            T.inc(ve.tensor_copy(stats[0:ST, SCR + 45:SCR + 46], pp[0:ST, 2:3]), "ve")
        blk.vector(vcp)

        def tsum2(te):
            T.wait(te, "te", "ve", T.cnt["ve"])
            T.sw(te, "te")
            te.matmul(pp[0:1, 40:41], stats[0:KC, SCR + 44:SCR + 45], ones_f[0:KC, 0:1],
                      start=True, stop=True)
            T.inc(te.matmul(pp[0:1, 42:43], stats[0:ST, SCR + 45:SCR + 46],
                            ones_f[0:ST, 0:1], start=True, stop=True), "te")
        blk.tensor(tsum2)

        def vsc(ve):
            T.wait(ve, "ve", "te", T.cnt["te"])
            T.sw(ve, "ve")
            T.inc(ve.tensor_copy(stats[0:1, SC1:SC1 + 1], pp[0:1, 40:41]), "ve")
            T.sw(ve, "ve")
            T.inc(ve.tensor_copy(stats[0:1, SC2_:SC2_ + 1], pp[0:1, 42:43]), "ve")
            T.sw(ve, "ve")
            T.inc(ve.tensor_scalar(stats[0:1, SC1:SC2_ + 1], stats[0:1, SC1:SC2_ + 1],
                                   1.0 / (1 << 22), 1e-5, ALU.mult, ALU.max), "ve")
        blk.vector(vsc)

        def tb(te):
            T.wait(te, "te", "ve", T.cnt["ve"])
            T.sw(te, "te")
            te.matmul(pp[0:P, 44:45], onesrow_f[:], stats[0:1, SC1:SC1 + 1],
                      start=True, stop=True)
            T.inc(te.matmul(pp[0:P, 46:47], onesrow_f[:], stats[0:1, SC2_:SC2_ + 1],
                            start=True, stop=True), "te")
        blk.tensor(tb)

        def vb(ve):
            T.wait(ve, "ve", "te", T.cnt["te"])
            T.sw(ve, "ve")
            T.inc(ve.tensor_scalar_mul(stats[:, THR1:THR1 + 1], pp[:, 44:45], 0.5), "ve")
            T.inc(ve.tensor_scalar_mul(stats[:, THR2:THR2 + 1], pp[:, 46:47], 0.5), "ve")
            T.inc(ve.tensor_scalar_mul(stats[:, NTH1:NTH1 + 1], pp[:, 44:45], -0.5), "ve")
            T.inc(ve.tensor_scalar_mul(stats[:, NTH2:NTH2 + 1], pp[:, 46:47], -0.5), "ve")
            T.inc(ve.tensor_scalar_mul(stats[:, M1C:M1C + 1], pp[:, 44:45], 1.0 / 127.0),
                  "ve")
            T.inc(ve.tensor_scalar_mul(stats[:, M2C:M2C + 1], pp[:, 46:47], 0.5 / 127.0),
                  "ve")
        blk.vector(vb)

        thr_ve = T.cnt["ve"]
        tern_tick = []
        ld2_tick = {}

        def syl2_one(eng, ename, qname, i):
            if i < 3:
                T.wait(eng, ename, "ve", thr_ve)
            if i >= 3:
                T.wait(eng, ename, "ve", tern_tick[i - 3])
            ld2_tick[i] = T.dma(eng, ename, qname, out=bufs[i % 3][:],
                                in_=w[f"w1t{L}"][i * P:(i + 1) * P, :])

        def vt1_chunk(ve, lo, hi):
            for i in range(lo, hi):
                T.wait(ve, "ve", "dls" if i % 2 == 0 else "dlg", ld2_tick[i])
                b = bufs[i % 3]
                T.sw(ve, "ve")
                T.inc(ve.tensor_scalar(wng[:], b[:], stats[:, NTH1:NTH1 + 1], None,
                                       ALU.is_lt), "ve")
                T.sw(ve, "ve")
                tern_tick.append(
                    T.inc(ve.scalar_tensor_tensor(wsl[:, i * DH:(i + 1) * DH], b[:],
                                                  stats[:, THR1:THR1 + 1], wng[:],
                                                  ALU.is_gt, ALU.subtract), "ve"))

        for lo in range(0, KC, 2):
            hi = min(lo + 2, KC)
            blk.sync(lambda sy_, lo=lo: syl2_one(sy_, "sy", "dls", lo))
            blk.gpsimd(lambda ge, lo=lo: syl2_one(ge, "ge", "dlg", lo + 1))
            blk.vector(lambda ve, lo=lo, hi=hi: vt1_chunk(ve, lo, hi))

        def tws(te):
            T.wait(te, "te", "ve", T.cnt["ve"])
            T.sw(te, "te")
            for c in range(ST):
                for k in range(KC):
                    ins = te.matmul(pp[:, 64 + c:65 + c],
                                    wsl[:, k * DH + c * P:k * DH + (c + 1) * P],
                                    ones_bf[:, 0:1], start=(k == 0), stop=(k == KC - 1))
                    if k == KC - 1:
                        T.inc(ins, "te")
        blk.tensor(tws)

        def vws(ve):
            T.wait(ve, "ve", "te", T.cnt["te"])
            T.sw(ve, "ve")
            T.inc(ve.tensor_scalar_mul(stats[:, NW1:NW1 + ST], pp[:, 64:64 + ST], -128.0),
                  "ve")
        blk.vector(vws)

        def sypark(sy_):
            T.wait(sy_, "sy", "ve", T.cnt["ve"])
            for hhf in range(2):
                for k in range(KC):
                    T.dma(sy_, "sy", "dss", out=g_wq[0][:, hhf * 16384 + k * 2048:hhf * 16384 + (k + 1) * 2048],
                        in_=wsl[:, k * DH + hhf * 2048:k * DH + (hhf + 1) * 2048])
        blk.sync(sypark)
        park1 = T.cnt["dss"]

        tern2_tick = []
        ld3_tick = {}

        def syl3_one(eng, ename, qname, j):
            if j < 3:
                T.wait(eng, ename, "dss", park1)
            if j >= 3:
                T.wait(eng, ename, "ve", tern2_tick[j - 3])
            ld3_tick[j] = T.dma(eng, ename, qname, out=bufs[j % 3][:, 0:D],
                                in_=w[f"w2t{L}"][j * P:(j + 1) * P, :])

        def vt2_chunk(ve, lo, hi):
            if lo == 0:
                T.wait(ve, "ve", "dss", park1)
            for j in range(lo, hi):
                T.wait(ve, "ve", "dls" if j % 2 == 0 else "dlg", ld3_tick[j])
                b = bufs[j % 3]
                T.sw(ve, "ve")
                T.inc(ve.tensor_scalar(wng[:, 0:D], b[:, 0:D], stats[:, NTH2:NTH2 + 1],
                                       None, ALU.is_lt), "ve")
                T.sw(ve, "ve")
                tern2_tick.append(
                    T.inc(ve.scalar_tensor_tensor(wsl[:, j * D:(j + 1) * D], b[:, 0:D],
                                                  stats[:, THR2:THR2 + 1], wng[:, 0:D],
                                                  ALU.is_gt, ALU.subtract), "ve"))

        for lo in range(0, ST, 2):
            hi = min(lo + 2, ST)
            blk.sync(lambda sy_, lo=lo: syl3_one(sy_, "sy", "dls", lo))
            blk.gpsimd(lambda ge, lo=lo: syl3_one(ge, "ge", "dlg", lo + 1))
            blk.vector(lambda ve, lo=lo, hi=hi: vt2_chunk(ve, lo, hi))

        def tws2(te):
            T.wait(te, "te", "ve", T.cnt["ve"])
            T.sw(te, "te")
            for nb in range(2):
                for c in range(ST):
                    ins = te.matmul(ppr[0:1, nb * 512:(nb + 1) * 512], ones_bf[:, 0:1],
                                    wsl[:, c * D + nb * 512:c * D + (nb + 1) * 512],
                                    start=(c == 0), stop=(c == ST - 1))
                    if c == ST - 1:
                        T.inc(ins, "te")
        blk.tensor(tws2)

        def vws2(ve):
            T.wait(ve, "ve", "te", T.cnt["te"])
            for nb in range(2):
                T.sw(ve, "ve")
                T.inc(ve.tensor_copy(row2_f[0:1, nb * 512:(nb + 1) * 512],
                                     ppr[0:1, nb * 512:(nb + 1) * 512]), "ve")
        blk.vector(vws2)

        def sypark2(sy_):
            T.wait(sy_, "sy", "ve", T.cnt["ve"])
            for hh in range(4):
                T.dma(sy_, "sy", "dss", out=g_wq[1][:, hh * 8192:(hh + 1) * 8192],
                                    in_=wsl[:, hh * 8192:(hh + 1) * 8192])
        blk.sync(sypark2)

    def ln_stats(snap0, scr):
        def vst(ve):
            T.wait_all(ve, "ve", snap0)
            for i in range(TT):
                T.inc(ve.tensor_reduce(stats[:, MEAN + i:MEAN + i + 1],
                                       xres[:, i * D:(i + 1) * D], AX.X, ALU.add), "ve")
        blk.vector(vst)

        def sst(se):
            T.wait_all(se, "se", snap0)
            for i in range(TT):
                T.sw(se, "se")
                T.inc(se.activation(scr[:], xres[:, i * D:(i + 1) * D], AF.Square,
                                    accum_out=stats[:, VAR + i:VAR + i + 1]), "se")
        blk.scalar(sst)

        def vrs(ve):
            T.wait(ve, "ve", "se", T.cnt["se"])
            T.sw(ve, "ve")
            T.inc(ve.tensor_scalar_mul(stats[:, MEAN:MEAN + TT], stats[:, MEAN:MEAN + TT],
                                       -1.0 / D), "ve")
            T.sw(ve, "ve")
            T.inc(ve.tensor_scalar_mul(stats[:, VAR:VAR + TT], stats[:, VAR:VAR + TT],
                                       1.0 / D), "ve")
            T.sw(ve, "ve")
            T.inc(ve.tensor_tensor(stats[:, SCR:SCR + TT], stats[:, MEAN:MEAN + TT],
                                   stats[:, MEAN:MEAN + TT], ALU.mult), "ve")
            T.sw(ve, "ve")
            T.inc(ve.tensor_tensor(stats[:, VAR:VAR + TT], stats[:, VAR:VAR + TT],
                                   stats[:, SCR:SCR + TT], ALU.subtract), "ve")
            T.sw(ve, "ve")
            T.inc(ve.tensor_single_scalar(stats[:, VAR:VAR + TT], stats[:, VAR:VAR + TT],
                                          1e-5, ALU.add), "ve")
        blk.vector(vrs)

        def srs(se):
            T.wait(se, "se", "ve", T.cnt["ve"])
            T.sw(se, "se")
            T.inc(se.activation(stats[:, RSTD:RSTD + TT], stats[:, VAR:VAR + TT],
                                AF.Sqrt), "se")
        blk.scalar(srs)

        def vnw(ve):
            T.wait(ve, "ve", "se", T.cnt["se"])
            T.sw(ve, "ve")
            T.inc(ve.reciprocal(stats[:, SCR:SCR + TT], stats[:, RSTD:RSTD + TT]), "ve")
            T.sw(ve, "ve")
            T.inc(ve.tensor_tensor(stats[:, SCR + 16:SCR + 32], stats[:, VAR:VAR + TT],
                                   stats[:, SCR:SCR + TT], ALU.mult), "ve")
            T.sw(ve, "ve")
            T.inc(ve.tensor_tensor(stats[:, SCR + 16:SCR + 32],
                                   stats[:, SCR + 16:SCR + 32],
                                   stats[:, SCR:SCR + TT], ALU.mult), "ve")
            T.sw(ve, "ve")
            T.inc(ve.tensor_scalar(stats[:, SCR + 16:SCR + 32],
                                   stats[:, SCR + 16:SCR + 32],
                                   -0.5, 1.5, ALU.mult, ALU.add), "ve")
            T.sw(ve, "ve")
            T.inc(ve.tensor_tensor(stats[:, RSTD:RSTD + TT], stats[:, SCR:SCR + TT],
                                   stats[:, SCR + 16:SCR + 32], ALU.mult), "ve")
        blk.vector(vnw)

    def ffn_call(L, tag):
        snap0 = T.snap()
        with contextlib.ExitStack() as ces:
            aS = ces.enter_context(contextlib.ExitStack())
            xqT = aS.enter_context(nc.sbuf_tensor(f"xqTF{tag}", [P, KC * S], BF16))
            alB = aS.enter_context(nc.sbuf_tensor(f"alBF{tag}", [P, S], F32))
            wslA = aS.enter_context(nc.sbuf_tensor(f"wslA{tag}", [P, 16384], BF16))

            def syw(sy_):
                T.wait_all(sy_, "sy", snap0)
                for hh in range(2):
                    T.dma(sy_, "sy", "dls", out=wslA[:, hh * 8192:(hh + 1) * 8192],
                                        in_=g_wq[0][:, hh * 8192:(hh + 1) * 8192])
            blk.sync(syw)
            w1_loaded = T.cnt["dls"]

            alB_ready = None
            se_cp = []
            with contextlib.ExitStack() as a0:
                xt0 = a0.enter_context(nc.sbuf_tensor(f"xt0{tag}", [P, D], F32))
                xt1 = a0.enter_context(nc.sbuf_tensor(f"xt1{tag}", [P, D], F32))
                r0x = a0.enter_context(nc.sbuf_tensor(f"r0x{tag}", [P, D], I32))
                qxb = a0.enter_context(nc.sbuf_tensor(f"qxb{tag}", [P, D], BF16))
                pmT = a0.enter_context(nc.psum_tensor(f"pmT{tag}", [P, D], BF16))
                prow = a0.enter_context(nc.psum_tensor(f"prow{tag}", [1, S], F32))
                pbc = a0.enter_context(nc.psum_tensor(f"pbc{tag}", [P, D], F32))

                ln_stats(snap0, xt0)

                vq_tick, tp_tick = [], []
                for i in range(TT):
                    def vq(ve, i=i):
                        if i >= 1:
                            T.wait(ve, "ve", "te", tp_tick[i - 1])
                        T.sw(ve, "ve")
                        T.inc(ve.tensor_scalar(xt0[:], xres[:, i * D:(i + 1) * D],
                                               stats[:, MEAN + i:MEAN + i + 1],
                                               stats[:, RSTD + i:RSTD + i + 1],
                                               ALU.add, ALU.mult), "ve")
                        T.sw(ve, "ve")
                        T.inc(ve.tensor_reduce(stats[:, XMAX + i:XMAX + i + 1], xt0[:],
                                               AX.X, ALU.max, apply_absolute_value=True),
                              "ve")
                        T.sw(ve, "ve")
                        T.inc(ve.tensor_single_scalar(stats[:, XMAX + i:XMAX + i + 1],
                                                      stats[:, XMAX + i:XMAX + i + 1],
                                                      1e-5, ALU.max), "ve")
                        T.sw(ve, "ve")
                        T.inc(ve.reciprocal(stats[:, SCR + 40:SCR + 41],
                                            stats[:, XMAX + i:XMAX + i + 1]), "ve")
                        T.sw(ve, "ve")
                        T.inc(ve.tensor_scalar_mul(stats[:, XS + i:XS + i + 1],
                                                   stats[:, SCR + 40:SCR + 41], 127.0),
                              "ve")
                        T.sw(ve, "ve")
                        T.inc(ve.tensor_scalar(xt1[:], xt0[:],
                                               stats[:, XS + i:XS + i + 1], 128.5,
                                               ALU.mult, ALU.add), "ve")
                        T.sw(ve, "ve")
                        T.inc(ve.tensor_single_scalar(r0x[:], xt1[:], RB, ALU.add), "ve")
                        T.sw(ve, "ve")
                        T.inc(ve.tensor_tensor(xt0[:], xt1[:], r0x[:], ALU.subtract),
                              "ve")
                        T.sw(ve, "ve")
                        T.inc(ve.tensor_single_scalar(xt0[:], xt0[:], 1.0, ALU.is_ge),
                              "ve")
                        T.sw(ve, "ve")
                        vq_tick.append(
                            T.inc(ve.scalar_tensor_tensor(qxb[:], xt0[:], 1.0, r0x[:],
                                                          ALU.mult, ALU.add), "ve"))
                    blk.vector(vq)

                    def tq(te, i=i):
                        T.wait(te, "te", "ve", vq_tick[i])
                        if i >= 1:
                            T.wait(te, "te", "se", se_cp[i - 1])
                        for k in range(KC):
                            ins = te.transpose(pmT[:, k * P:(k + 1) * P],
                                               qxb[:, k * P:(k + 1) * P], ident_bf[:])
                            if k == KC - 1:
                                tp_tick.append(T.inc(ins, "te"))
                    blk.tensor(tq)

                    def sq(se, i=i):
                        T.wait(se, "se", "te", tp_tick[i])
                        T.sw(se, "se")
                        for k in range(KC):
                            ins = se.copy(xqT[:, k * S + i * P:k * S + (i + 1) * P],
                                          pmT[:, k * P:(k + 1) * P])
                            if k == KC - 1:
                                se_cp.append(T.inc(ins, "se"))
                    blk.scalar(sq)

                def va(ve):
                    T.sw(ve, "ve")
                    T.inc(ve.tensor_scalar(stats[:, ALPHA:ALPHA + TT],
                                           stats[:, XMAX:XMAX + TT],
                                           stats[:, M1C:M1C + 1], None, ALU.mult), "ve")
                blk.vector(va)
                a_ve = T.cnt["ve"]

                def ta(te):
                    T.wait(te, "te", "ve", a_ve)
                    T.sw(te, "te")
                    for i in range(TT):
                        ins = te.matmul(prow[0:1, i * P:(i + 1) * P],
                                        stats[:, ALPHA + i:ALPHA + i + 1], ident_f[:],
                                        start=True, stop=True)
                        if i == TT - 1:
                            T.inc(ins, "te")
                blk.tensor(ta)
                a_te = T.cnt["te"]

                for hh in range(2):
                    def scp(se, hh=hh):
                        T.wait(se, "se", "te", T.cnt["te"])
                        for nb in range(2):
                            T.sw(se, "se")
                            T.inc(se.copy(
                                borow[0:1, nb * 512:(nb + 1) * 512],
                                prow[0:1, hh * D + nb * 512:hh * D + (nb + 1) * 512]),
                                "se")
                    blk.scalar(scp)

                    def tbc(te, hh=hh):
                        T.wait(te, "te", "se", T.cnt["se"])
                        T.wait(te, "te", "ve", T.cnt["ve"])
                        T.sw(te, "te")
                        te.matmul(pbc[:, 0:512], onesrow_f[:], borow[0:1, 0:512],
                                  start=True, stop=True)
                        T.inc(te.matmul(pbc[:, 512:1024], onesrow_f[:],
                                        borow[0:1, 512:1024], start=True, stop=True),
                              "te")
                    blk.tensor(tbc)

                    def vbc(ve, hh=hh):
                        T.wait(ve, "ve", "te", T.cnt["te"])
                        for nb in range(2):
                            T.sw(ve, "ve")
                            T.inc(ve.tensor_copy(
                                alB[:, hh * D + nb * 512:hh * D + (nb + 1) * 512],
                                pbc[:, nb * 512:(nb + 1) * 512]), "ve")
                    blk.vector(vbc)
                alB_ready = T.cnt["ve"]

            if FFN_STAGE < 2:
                return
            snapA = T.snap()
            with contextlib.ExitStack() as a1:
                sA = a1.enter_context(nc.sbuf_tensor(f"sA{tag}", [P, S], F32))
                sZ0 = a1.enter_context(nc.sbuf_tensor(f"sZ0{tag}", [P, S], F32))
                sZ1 = a1.enter_context(nc.sbuf_tensor(f"sZ1{tag}", [P, S], F32))
                acc = a1.enter_context(nc.sbuf_tensor(f"acc{tag}", [P, S], F32))
                zsq0 = a1.enter_context(nc.sbuf_tensor(f"zsq0{tag}", [P, S], F32))
                zsq1 = a1.enter_context(nc.sbuf_tensor(f"zsq1{tag}", [P, S], F32))
                pmA = a1.enter_context(nc.psum_tensor(f"pmA{tag}", [P, S], F32))
                pmB = a1.enter_context(nc.psum_tensor(f"pmB{tag}", [P, S], F32))
                pms = [pmA, pmB]
                sZ = [sZ0, sZ1]
                zsq = [zsq0, zsq1]

                def vinit(ve):
                    T.wait_all(ve, "ve", snapA)
                    T.sw(ve, "ve")
                    T.inc(ve.memset(acc[:], 0.0), "ve")
                blk.vector(vinit)
                acc_init = T.cnt["ve"]
                ge_snapA = T.snap()

                mm_t, ev_t, base_t, sh_t, silu_t, dss_t = [], [], [], [], [], []
                gq_t, mx_t = [], []
                half_loaded = [w1_loaded, None]
                for c in range(ST):
                    if c == 16:
                        def syw1b(sy_):
                            T.wait(sy_, "sy", "te", mm_t[15])
                            for hh in range(2):
                                T.dma(sy_, "sy", "dls", out=wslA[:, hh * 8192:(hh + 1) * 8192],
                                    in_=g_wq[0][:, 16384 + hh * 8192:
                                                16384 + (hh + 1) * 8192])
                        blk.sync(syw1b)
                        half_loaded[1] = T.cnt["dls"]

                    def tA(te, c=c):
                        if c == 0:
                            T.wait_all(te, "te", snapA)
                        T.wait(te, "te", "dls", half_loaded[c // 16])
                        if c >= 2:
                            T.wait(te, "te", "se", ev_t[c - 2])
                        pm = pms[c % 2]
                        cc = c % 16
                        for nb in range(4):
                            for k in range(KC):
                                ins = te.matmul(
                                    pm[:, nb * 512:(nb + 1) * 512],
                                    wslA[:, k * 2048 + cc * P:k * 2048 + (cc + 1) * P],
                                    xqT[:, k * S + nb * 512:k * S + (nb + 1) * 512],
                                    start=(k == 0), stop=(k == KC - 1))
                        mm_t.append(T.inc(ins, "te"))
                    blk.tensor(tA)

                    def scp1(se, c=c):
                        T.wait(se, "se", "te", mm_t[c])
                        if c >= 1:
                            T.wait(se, "se", "ve", sh_t[c - 1])
                        ins = None
                        for nb in range(4):
                            T.sw(se, "se")
                            ins = se.copy(sA[:, nb * 512:(nb + 1) * 512],
                                          pms[c % 2][:, nb * 512:(nb + 1) * 512])
                        ev_t.append(T.inc(ins, "se"))
                    blk.scalar(scp1)

                    def v1(ve, c=c):
                        T.wait(ve, "ve", "se", ev_t[c])
                        T.sw(ve, "ve")
                        ins = T.inc(ve.scalar_tensor_tensor(
                            sA[:], sA[:], stats[:, NW1 + c:NW1 + c + 1],
                            alB[:], ALU.add, ALU.mult), "ve")
                        if A1_LEVEL < 1:
                            sh_t.append(ins)
                            return
                        z = sZ[c % 2]
                        if c >= 2 and A1_LEVEL >= 2:
                            T.wait(ve, "ve", "se", silu_t[c - 2])
                            T.wait(ve, "ve", "dss", dss_t[c - 2])
                        if c >= 2 and A1_LEVEL >= 3:
                            T.wait(ve, "ve", "ge", gq_t[c - 2])
                        T.sw(ve, "ve")
                        T.inc(ve.tensor_scalar(z[:], sA[:],
                                               convw_sb[:, 3 * c + 1:3 * c + 2],
                                               convb_sb[:, c:c + 1],
                                               ALU.mult, ALU.add), "ve")
                        T.sw(ve, "ve")
                        T.inc(ve.scalar_tensor_tensor(z[:, 1:S], sA[:, 0:S - 1],
                                                      convw_sb[:, 3 * c:3 * c + 1],
                                                      z[:, 1:S], ALU.mult, ALU.add),
                              "ve")
                        T.sw(ve, "ve")
                        sh_t.append(
                            T.inc(ve.scalar_tensor_tensor(z[:, 0:S - 1], sA[:, 1:S],
                                                          convw_sb[:, 3 * c + 2:3 * c + 3],
                                                          z[:, 0:S - 1], ALU.mult,
                                                          ALU.add), "ve"))
                        if c >= 2 and A1_LEVEL >= 3:
                            T.wait(ve, "ve", "ve", acc_init)
                            T.wait(ve, "ve", "ge", gq_t[c - 2])
                            T.sw(ve, "ve")
                            mx_t.append(T.inc(ve.tensor_tensor(
                                acc[:], acc[:], zsq[c % 2][:], ALU.max), "ve"))
                    blk.vector(v1)

                    def s2(se, cc=c - 1):
                        if A1_LEVEL < 2 or cc < 0:
                            return
                        T.wait(se, "se", "ve", sh_t[cc])
                        z = sZ[cc % 2]
                        T.sw(se, "se")
                        silu_t.append(T.inc(se.activation(z[:], z[:], AF.Silu), "se"))
                        T.sw(se, "se")
                        dss_t.append(T.dma(se, "se", "dss", out=g_silu[cc], in_=z[:]))
                    blk.scalar(s2)

                    def gq1(ge, cc=c - 1):
                        if A1_LEVEL < 3 or cc < 0:
                            return
                        if cc < 2:
                            T.wait_all(ge, "ge", ge_snapA)
                        T.wait(ge, "ge", "se", silu_t[cc])
                        if cc >= 2:
                            T.wait(ge, "ge", "ve", mx_t[cc - 2])
                        gq_t.append(T.inc(ge.tensor_tensor(
                            zsq[cc % 2][:], sZ[cc % 2][:], sZ[cc % 2][:], ALU.mult),
                            "ge"))
                    blk.gpsimd(gq1)

                def s2f(se):
                    if A1_LEVEL < 2:
                        return
                    cc = ST - 1
                    T.wait(se, "se", "ve", sh_t[cc])
                    z = sZ[cc % 2]
                    T.sw(se, "se")
                    silu_t.append(T.inc(se.activation(z[:], z[:], AF.Silu), "se"))
                    T.sw(se, "se")
                    dss_t.append(T.dma(se, "se", "dss", out=g_silu[cc], in_=z[:]))
                blk.scalar(s2f)

                def gq1f(ge):
                    if A1_LEVEL < 3:
                        return
                    cc = ST - 1
                    T.wait(ge, "ge", "se", silu_t[cc])
                    T.wait(ge, "ge", "ve", mx_t[cc - 2])
                    gq_t.append(T.inc(ge.tensor_tensor(
                        zsq[cc % 2][:], sZ[cc % 2][:], sZ[cc % 2][:], ALU.mult), "ge"))
                blk.gpsimd(gq1f)

                def vmaxf(ve):
                    if A1_LEVEL < 3:
                        return
                    for cc in (ST - 2, ST - 1):
                        T.wait(ve, "ve", "ge", gq_t[cc])
                        T.sw(ve, "ve")
                        mx_t.append(T.inc(ve.tensor_tensor(
                            acc[:], acc[:], zsq[cc % 2][:], ALU.max), "ve"))
                blk.vector(vmaxf)


                tp2_t, red_t = [], []
                for i in range(TT if A1_LEVEL >= 3 else 0):
                    def tmx(te, i=i):
                        if i == 0:
                            T.wait(te, "te", "ve", mx_t[ST - 1])
                            T.wait(te, "te", "se", ev_t[ST - 1])
                        if i >= 2:
                            T.wait(te, "te", "ve", red_t[i - 2])
                        tp2_t.append(
                            T.inc(te.transpose(
                                pmA[:, (i % 2) * 512:(i % 2) * 512 + P],
                                acc[:, i * P:(i + 1) * P], ident_f[:]), "te"))
                    blk.tensor(tmx)

                    def vmx(ve, i=i):
                        T.wait(ve, "ve", "te", tp2_t[i])
                        T.sw(ve, "ve")
                        T.inc(ve.tensor_copy(
                            sA[:, 0:P],
                            pmA[:, (i % 2) * 512:(i % 2) * 512 + P]), "ve")
                        T.sw(ve, "ve")
                        red_t.append(
                            T.inc(ve.tensor_reduce(
                                stats[:, SC2 + i:SC2 + i + 1],
                                sA[:, 0:P], AX.X, ALU.max), "ve"))
                    blk.vector(vmx)

                if A1_LEVEL >= 3:
                    def sfsq(se):
                        T.wait(se, "se", "ve", red_t[TT - 1])
                        T.sw(se, "se")
                        T.inc(se.activation(stats[:, SCR + 64:SCR + 64 + TT],
                                            stats[:, SC2:SC2 + TT], AF.Sqrt), "se")
                    blk.scalar(sfsq)
                sq_se = T.cnt["se"]

                def vfin(ve):
                    if A1_LEVEL < 3:
                        return
                    T.wait(ve, "ve", "se", sq_se)
                    T.sw(ve, "ve")
                    T.inc(ve.reciprocal(stats[:, SCR + 80:SCR + 80 + TT],
                                        stats[:, SCR + 64:SCR + 64 + TT]), "ve")
                    T.sw(ve, "ve")
                    T.inc(ve.tensor_tensor(stats[:, SCR + 80:SCR + 80 + TT],
                                           stats[:, SCR + 80:SCR + 80 + TT],
                                           stats[:, SC2:SC2 + TT], ALU.mult), "ve")
                    T.sw(ve, "ve")
                    T.inc(ve.tensor_tensor(stats[:, SCR + 80:SCR + 80 + TT],
                                           stats[:, SCR + 80:SCR + 80 + TT],
                                           stats[:, SCR + 64:SCR + 64 + TT], ALU.add),
                          "ve")
                    T.sw(ve, "ve")
                    T.inc(ve.tensor_scalar(stats[:, SC2:SC2 + TT],
                                           stats[:, SCR + 80:SCR + 80 + TT],
                                           0.5, 1e-5, ALU.mult, ALU.max), "ve")
                    T.sw(ve, "ve")
                    T.inc(ve.tensor_scalar(stats[:, BETA:BETA + TT],
                                           stats[:, SC2:SC2 + TT],
                                           stats[:, M2C:M2C + 1], None, ALU.mult), "ve")
                    T.sw(ve, "ve")
                    T.inc(ve.reciprocal(stats[:, SCR:SCR + TT], stats[:, SC2:SC2 + TT]),
                          "ve")
                    T.sw(ve, "ve")
                    T.inc(ve.tensor_scalar_mul(stats[:, SC2:SC2 + TT],
                                               stats[:, SCR:SCR + TT], 127.0), "ve")
                blk.vector(vfin)

            if FFN_STAGE < 3:
                return
            aS.close()
            snapB = T.snap()
            with contextlib.ExitStack() as b1:
                wsl = b1.enter_context(nc.sbuf_tensor(f"wslB{tag}", [P, 32768], BF16))

                def ssw2(se):
                    T.wait_all(se, "se", snapB)
                    for hh in range(4):
                        T.dma(se, "se", "dss", out=wsl[:, hh * 8192:(hh + 1) * 8192],
                                            in_=g_wq[1][:, hh * 8192:(hh + 1) * 8192])
                blk.scalar(ssw2)
                w2_loaded = T.cnt["dss"]
                sb2 = b1.enter_context(nc.sbuf_tensor(f"sb2{tag}", [P, S], F32))
                with contextlib.ExitStack() as b0:
                    prow2 = b0.enter_context(nc.psum_tensor(f"prw2{tag}", [1, S], F32))
                    pbc2 = b0.enter_context(nc.psum_tensor(f"pbc2{tag}", [P, D], F32))

                    def tr2(te):
                        T.wait_all(te, "te", snapB)
                        T.sw(te, "te")
                        for i in range(TT):
                            ins = te.matmul(prow2[0:1, i * P:(i + 1) * P],
                                            stats[:, SC2 + i:SC2 + i + 1], ident_f[:],
                                            start=True, stop=True)
                            if i == TT - 1:
                                T.inc(ins, "te")
                    blk.tensor(tr2)

                    for hh in range(2):
                        def scp2(se, hh=hh):
                            T.wait_all(se, "se", snapB)
                            T.wait(se, "se", "te", T.cnt["te"])
                            for nb in range(2):
                                T.sw(se, "se")
                                T.inc(se.copy(
                                    borow[0:1, nb * 512:(nb + 1) * 512],
                                    prow2[0:1, hh * D + nb * 512:
                                          hh * D + (nb + 1) * 512]), "se")
                        blk.scalar(scp2)

                        def tbc2(te, hh=hh):
                            T.wait(te, "te", "se", T.cnt["se"])
                            T.wait(te, "te", "ve", T.cnt["ve"])
                            T.sw(te, "te")
                            te.matmul(pbc2[:, 0:512], onesrow_f[:], borow[0:1, 0:512],
                                      start=True, stop=True)
                            T.inc(te.matmul(pbc2[:, 512:1024], onesrow_f[:],
                                            borow[0:1, 512:1024], start=True, stop=True),
                                  "te")
                        blk.tensor(tbc2)

                        def vbc2(ve, hh=hh):
                            T.wait_all(ve, "ve", snapB)
                            T.wait(ve, "ve", "te", T.cnt["te"])
                            for nb in range(2):
                                T.sw(ve, "ve")
                                T.inc(ve.tensor_copy(
                                    sb2[:, hh * D + nb * 512:hh * D + (nb + 1) * 512],
                                    pbc2[:, nb * 512:(nb + 1) * 512]), "ve")
                        blk.vector(vbc2)
                sb2_ready = T.cnt["ve"]

                rls = [b1.enter_context(nc.sbuf_tensor(f"rl{j}{tag}", [P, 1024], F32))
                       for j in range(3)]
                Ubs = [b1.enter_context(nc.sbuf_tensor(f"Ub{j}{tag}", [P, 1024], F32))
                       for j in range(2)]
                r0b = b1.enter_context(nc.sbuf_tensor(f"r0b{tag}", [P, 1024], I32))
                qsl = [b1.enter_context(nc.sbuf_tensor(f"qsl{j}{tag}", [P, 1024], BF16))
                       for j in range(2)]
                pmBq = [b1.enter_context(nc.psum_tensor(f"pmq{j}{tag}", [P, D], F32))
                        for j in range(4)]

                vq_t, mm2_t, res_t, fix_t, gm_t = {}, {}, {}, {}, {}
                NIT = 4 * ST // 2

                def syr(eng, ename, qname, it):
                    tbb, cp = it // (ST // 2), it % (ST // 2)
                    if it < 3:
                        T.wait_all(eng, ename, snapB)
                    if it >= 3:
                        T.wait(eng, ename, "ve", vq_t[it - 3][0])
                    for hh in range(2):
                        T.dma(eng, ename, qname,
                              out=rls[it % 3][:, hh * 512:(hh + 1) * 512],
                              in_=g_silu[2 * cp + hh][:,
                                                      tbb * 512:(tbb + 1) * 512])

                def gmul(ge, it, ld_tick):
                    tbb = it // (ST // 2)
                    if it == 0:
                        T.wait(ge, "ge", "ve", sb2_ready)
                    T.wait(ge, "ge", "dlg" if it % 2 == 0 else "dls", ld_tick)
                    if it >= 2:
                        T.wait(ge, "ge", "ve", vq_t[it - 2][1])
                    for hh in range(2):
                        if hh == 1:
                            T.sw(ge, "ge")
                        ins = ge.tensor_tensor(
                            Ubs[it % 2][:, hh * 512:(hh + 1) * 512],
                            rls[it % 3][:, hh * 512:(hh + 1) * 512],
                            sb2[:, tbb * 512:(tbb + 1) * 512], ALU.mult)
                    gm_t[it] = T.inc(ins, "ge")

                def vqb(ve, it):
                    T.wait(ve, "ve", "ge", gm_t[it])
                    if it >= 2:
                        T.wait(ve, "ve", "te", mm2_t[it - 2])
                    buf = rls[it % 3]
                    Ub = Ubs[it % 2]
                    T.sw(ve, "ve")
                    T.inc(ve.tensor_single_scalar(r0b[:], Ub[:], 128.4990234375,
                                                  ALU.add), "ve")
                    T.sw(ve, "ve")
                    T.inc(ve.scalar_tensor_tensor(
                        buf[:], Ub[:], 128.5, r0b[:], ALU.add, ALU.subtract), "ve")
                    T.sw(ve, "ve")
                    e_tick = T.inc(ve.tensor_single_scalar(
                        Ub[:], buf[:], 1.0, ALU.is_ge), "ve")
                    T.sw(ve, "ve")
                    q_tick = T.inc(ve.scalar_tensor_tensor(
                        qsl[it % 2][:], Ub[:], 1.0, r0b[:], ALU.mult, ALU.add), "ve")
                    vq_t[it] = (e_tick, q_tick)

                def tBc(te, it):
                    tbb, cp = it // (ST // 2), it % (ST // 2)
                    T.wait(te, "te", "ve", vq_t[it][1])
                    if it == 0:
                        T.wait(te, "te", "dss", w2_loaded)
                    if cp == 0 and tbb >= 1:
                        for sub in range(4):
                            T.wait(te, "te", "ve", res_t[(tbb - 1, sub)])
                    last = None
                    for hh in range(2):
                        c = 2 * cp + hh
                        for sub in range(4):
                            for nb in range(2):
                                last = te.matmul(
                                    pmBq[sub][:, nb * 512:(nb + 1) * 512],
                                    qsl[it % 2][:, hh * 512 + sub * P:
                                                hh * 512 + (sub + 1) * P],
                                    wsl[:, c * D + nb * 512:c * D + (nb + 1) * 512],
                                    start=(c == 0), stop=False)
                    mm2_t[it] = T.inc(last, "te")

                def emit_bfix(tbb):
                    for sub in range(4):
                        def tfix(te, tbb=tbb, sub=sub):
                            last = None
                            for nb in range(2):
                                last = te.matmul(
                                    pmBq[sub][:, nb * 512:(nb + 1) * 512],
                                    neg128row[:],
                                    row2_f[0:1, nb * 512:(nb + 1) * 512],
                                    start=False, stop=True)
                            fix_t[(tbb, sub)] = T.inc(last, "te")
                        blk.tensor(tfix)

                        def vres(ve, tbb=tbb, sub=sub):
                            tb = tbb * 4 + sub
                            T.wait(ve, "ve", "te", fix_t[(tbb, sub)])
                            last = None
                            for nb in range(2):
                                T.sw(ve, "ve")
                                last = T.inc(ve.scalar_tensor_tensor(
                                    xres[:, tb * D + nb * 512:tb * D + (nb + 1) * 512],
                                    pmBq[sub][:, nb * 512:(nb + 1) * 512],
                                    stats[:, BETA + tb:BETA + tb + 1],
                                    xres[:, tb * D + nb * 512:tb * D + (nb + 1) * 512],
                                    ALU.mult, ALU.add), "ve")
                            res_t[(tbb, sub)] = last
                        blk.vector(vres)

                ld_ticks = {}
                for g in range(NIT + 3):
                    if g >= 2 and g - 2 < NIT:
                        blk.vector(lambda ve, it=g - 2: vqb(ve, it))
                    if g < NIT:
                        if g % 2 == 0:
                            blk.gpsimd(lambda ge, it=g: syr(ge, "ge", "dlg", it))
                            ld_ticks[g] = T.cnt["dlg"]
                        else:
                            blk.sync(lambda sy_, it=g: syr(sy_, "sy", "dls", it))
                            ld_ticks[g] = T.cnt["dls"]
                    if g >= 1 and g - 1 < NIT:
                        blk.gpsimd(lambda ge, it=g - 1: gmul(ge, it, ld_ticks[g - 1]))
                    if g >= 3:
                        blk.tensor(lambda te, it=g - 3: tBc(te, it))
                        if (g - 3) % (ST // 2) == ST // 2 - 1:
                            emit_bfix((g - 3) // (ST // 2))

    def attn_lnT(snap0, ces):
        xlnT = ces.enter_context(nc.sbuf_tensor(f"xlnT{len(str(snap0))}{T.cnt['te']}",
                                                [P, KC * S], F32))
        with contextlib.ExitStack() as a0:
            xt0a = a0.enter_context(nc.sbuf_tensor(f"xtA{T.cnt['te']}", [P, D], F32))
            xt0b = a0.enter_context(nc.sbuf_tensor(f"xtB{T.cnt['te']}", [P, D], F32))
            pmTf = a0.enter_context(nc.psum_tensor(f"pmTA{T.cnt['te']}", [P, D], F32))
            xts = [xt0a, xt0b]
            ln_stats(snap0, xt0a)
            ln_t, tp_t, cp_t = [], [], []
            for i in range(TT):
                def vln(ve, i=i):
                    if i >= 2:
                        T.wait(ve, "ve", "te", tp_t[i - 2])
                    T.sw(ve, "ve")
                    ln_t.append(
                        T.inc(ve.tensor_scalar(xts[i % 2][:],
                                               xres[:, i * D:(i + 1) * D],
                                               stats[:, MEAN + i:MEAN + i + 1],
                                               stats[:, RSTD + i:RSTD + i + 1],
                                               ALU.add, ALU.mult), "ve"))
                blk.vector(vln)

                def tln(te, i=i):
                    T.wait(te, "te", "ve", ln_t[i])
                    if i >= 1:
                        T.wait(te, "te", "se", cp_t[i - 1])
                    for k in range(KC):
                        ins = te.transpose(pmTf[:, k * P:(k + 1) * P],
                                           xts[i % 2][:, k * P:(k + 1) * P],
                                           ident_f[:])
                        if k == KC - 1:
                            tp_t.append(T.inc(ins, "te"))
                blk.tensor(tln)

                def sln(se, i=i):
                    T.wait(se, "se", "te", tp_t[i])
                    T.sw(se, "se")
                    for k in range(KC):
                        ins = se.copy(xlnT[:, k * S + i * P:k * S + (i + 1) * P],
                                      pmTf[:, k * P:(k + 1) * P])
                        if k == KC - 1:
                            cp_t.append(T.inc(ins, "se"))
                blk.scalar(sln)
        return xlnT

    norm_state = {"prev_ve": 0}

    def emit_norm(psq_ap, prowD, nrm2, run_col, ready):
        for tb4 in range(4):
            def tn2(te, tb4=tb4, rd=ready, pv=norm_state["prev_ve"]):
                T.wait(te, "te", rd[0], rd[1])
                T.wait(te, "te", "ve", pv)
                T.sw(te, "te")
                T.inc(te.matmul(prowD[0:2, 0:512], stats[:, CA:CA + 2],
                                psq_ap[:, tb4 * 512:(tb4 + 1) * 512],
                                start=True, stop=True), "te")
            blk.tensor(tn2)

            def vn2(ve, tb4=tb4):
                T.wait(ve, "ve", "te", T.cnt["te"])
                T.sw(ve, "ve")
                T.inc(ve.tensor_copy(nrm2[0:2, 0:512], prowD[0:2, 0:512]), "ve")
                T.sw(ve, "ve")
                T.inc(ve.tensor_reduce(stats[0:2, SCR + 47:SCR + 48],
                                       nrm2[0:2, 0:512], AX.X, ALU.max), "ve")
                T.sw(ve, "ve")
                T.inc(ve.tensor_tensor(stats[0:2, run_col:run_col + 1],
                                       stats[0:2, run_col:run_col + 1],
                                       stats[0:2, SCR + 47:SCR + 48], ALU.max), "ve")
            blk.vector(vn2)
            norm_state["prev_ve"] = T.cnt["ve"]

    def emit_negM():
        def symrg(sy_):
            T.wait(sy_, "sy", "ve", T.cnt["ve"])
            T.dma(sy_, "sy", "dls", out=stats[0:1, SCR + 90:SCR + 92],
                  in_=stats[1:2, SC3:SC4 + 1])
        blk.sync(symrg)
        mrg_tick = T.cnt["dls"]

        def v1(ve):
            T.wait(ve, "ve", "dls", mrg_tick)
            T.sw(ve, "ve")
            T.inc(ve.tensor_tensor(stats[0:1, SC3:SC4 + 1],
                                   stats[0:1, SC3:SC4 + 1],
                                   stats[0:1, SCR + 90:SCR + 92], ALU.max), "ve")
            T.sw(ve, "ve")
            T.inc(ve.tensor_tensor(stats[0:1, SC1:SC1 + 1], stats[0:1, SC3:SC3 + 1],
                                   stats[0:1, SC4:SC4 + 1], ALU.mult), "ve")
        blk.vector(v1)

        def s1(se):
            T.wait(se, "se", "ve", T.cnt["ve"])
            T.sw(se, "se")
            T.inc(se.activation(stats[0:1, SC1:SC1 + 1], stats[0:1, SC1:SC1 + 1],
                                AF.Sqrt), "se")
        blk.scalar(s1)

        def v2(ve):
            T.wait(ve, "ve", "se", T.cnt["se"])
            T.sw(ve, "ve")
            T.inc(ve.tensor_scalar_mul(stats[0:1, SC1:SC1 + 1], stats[0:1, SC1:SC1 + 1],
                                       -0.125), "ve")
        blk.vector(v2)

    def emit_negM_bcast(pm_small):
        def t1(te):
            T.wait(te, "te", "ve", T.cnt["ve"])
            T.wait(te, "te", "se", T.cnt["se"])
            T.sw(te, "te")
            T.inc(te.matmul(pm_small[0:P, 0:1], onesrow_f[:], stats[0:1, SC1:SC1 + 1],
                            start=True, stop=True), "te")
        blk.tensor(t1)

        def v3(ve):
            T.wait(ve, "ve", "te", T.cnt["te"])
            T.sw(ve, "ve")
            T.inc(ve.tensor_copy(stats[:, NEGM:NEGM + 1], pm_small[0:P, 0:1]), "ve")
        blk.vector(v3)

    def emit_sc(L, ctx, kb_getter, vf_getter, load_pair):
        with contextlib.ExitStack() as ces:
            qb0 = ces.enter_context(nc.sbuf_tensor(f"qb0L{L}", [P, S], F32))
            qb1 = ces.enter_context(nc.sbuf_tensor(f"qb1L{L}", [P, S], F32))
            pr = [[ces.enter_context(nc.sbuf_tensor(f"pr{h}{pp}L{L}", [P, 256], F32))
                   for pp in range(2)] for h in range(2)]
            psS = [ces.enter_context(nc.psum_tensor(f"psS{j}L{L}", [P, 512], F32))
                   for j in range(4)]
            psC = [ces.enter_context(nc.psum_tensor(f"psC{j}L{L}", [P, 512], F32))
                   for j in range(4)]
            qbs = [qb0, qb1]
            qb_tick = [0, 0]
            pair_te_end = [0] * NPAIR
            prev_evac = 0

            sc_snap0 = T.snap()
            for p in range(NPAIR):
                def syq(sy_, p=p):
                    if p < 2:
                        T.wait_all(sy_, "sy", sc_snap0)
                    else:
                        T.wait(sy_, "sy", "te", pair_te_end[p - 2])
                    load_pair(sy_, p, qbs[p % 2])
                blk.sync(syq)
                qb_tick[p % 2] = T.cnt["dls"]
                kb, kb_tick, vf, vf_tick = kb_getter(p), None, vf_getter(p), None

                for qc in range(8):
                    n = 2 * qc + 2
                    te_base = T.cnt["te"]
                    se_base = T.cnt["se"]
                    ve_base = T.cnt["ve"]
                    my_prev_evac = prev_evac
                    my_qb_tick = qb_tick[p % 2]

                    def tsc(te, p=p, qc=qc, n=n, te_base=te_base, se_base=se_base,
                            ve_base=ve_base, my_prev_evac=my_prev_evac,
                            my_qb_tick=my_qb_tick, kb=kb, vf=vf):
                        qb = qbs[p % 2]
                        if p == 0 and qc == 0:
                            T.wait_all(te, "te", sc_snap0)

                        def emit_ctx(j):
                            T.wait(te, "te", "se", se_base + 2 * (j + 1))
                            if j >= 2 * qc:
                                T.wait(te, "te", "ve", ve_base + 2 * (j - 2 * qc + 1))
                            if j == 0:
                                T.wait(te, "te", "ve", my_prev_evac)
                            last = None
                            for qt_local in range(2):
                                qt = 2 * qc + qt_local
                                if qt < j:
                                    continue
                                for hd in range(2):
                                    last = te.matmul(
                                        psC[2 * qt_local + hd][:, 0:65],
                                        pr[hd][j % 2][:, qt_local * P:(qt_local + 1) * P],
                                        vf[hd][:, j * 65:j * 65 + 65],
                                        start=(j == 0), stop=(j == qt))
                            T.inc(last, "te")

                        for v in range(n):
                            if v == 0:
                                T.wait(te, "te", "dls", my_qb_tick)
                                T.wait(te, "te", "se", se_base)
                            if v >= 2:
                                T.wait(te, "te", "se", se_base + 2 * (v - 1))
                            for hd in range(2):
                                ins = te.matmul(
                                    psS[2 * (v % 2) + hd][:, 0:256],
                                    kb[hd * DK:(hd + 1) * DK, v * P:(v + 1) * P],
                                    qb[hd * DK:(hd + 1) * DK, qc * 256:(qc + 1) * 256],
                                    start=True, stop=True, tile_position=(64 * hd, 0))
                                if hd == 1:
                                    T.inc(ins, "te")
                            if v >= 1:
                                emit_ctx(v - 1)
                        emit_ctx(n - 1)
                    blk.tensor(tsc)

                    def ssc(se, p=p, qc=qc, n=n, te_base=te_base):
                        for v in range(n):
                            T.wait(se, "se", "te",
                                   te_base + (1 if v == 0 else 2 * v))
                            T.sw(se, "se")
                            for hd in range(2):
                                T.inc(se.activation(pr[hd][v % 2][:],
                                                    psS[2 * (v % 2) + hd][:, 0:256],
                                                    AF.Exp,
                                                    bias=stats[:, NEGM:NEGM + 1],
                                                    scale=0.125), "se")
                    blk.scalar(ssc)

                    def vsc(ve, p=p, qc=qc, n=n, te_base=te_base, se_base=se_base):
                        for v in (2 * qc, 2 * qc + 1):
                            qt_local = v - 2 * qc
                            T.wait(ve, "ve", "se", se_base + 2 * (v + 1))
                            for hd in range(2):
                                T.sw(ve, "ve")
                                T.inc(ve.tensor_tensor(
                                    pr[hd][v % 2][:, qt_local * P:(qt_local + 1) * P],
                                    pr[hd][v % 2][:, qt_local * P:(qt_local + 1) * P],
                                    maskt[:], ALU.mult), "ve")
                        for qt_local in range(2):
                            qt = 2 * qc + qt_local
                            T.wait(ve, "ve", "te",
                                   te_base + (2 * qt + 3 if qt <= n - 2
                                              else 2 * n))
                            for hd in range(2):
                                slot = psC[2 * qt_local + hd]
                                T.sw(ve, "ve")
                                T.inc(ve.reciprocal(stats[:, SCR + 48:SCR + 49],
                                                    slot[:, DK:DK + 1]), "ve")
                                T.sw(ve, "ve")
                                T.inc(ve.tensor_scalar(
                                    ctx[:, qt * D + p * P + hd * DK:
                                        qt * D + p * P + (hd + 1) * DK],
                                    slot[:, 0:DK], stats[:, SCR + 48:SCR + 49], None,
                                    ALU.mult), "ve")
                    blk.vector(vsc)
                    prev_evac = T.cnt["ve"]
                pair_te_end[p] = T.cnt["te"]

    def emit_wo(ctx, wo_ap, has_bias):
        snap0 = T.snap()
        with contextlib.ExitStack() as ces:
            woS = ces.enter_context(nc.sbuf_tensor(f"woS{T.cnt['te']}", [P, 8192], F32))
            ctT = ces.enter_context(nc.sbuf_tensor(f"ctT{T.cnt['te']}", [P, D], F32))
            psT = ces.enter_context(nc.psum_tensor(f"psT{T.cnt['te']}", [P, D], F32))
            pmW0 = ces.enter_context(nc.psum_tensor(f"pmW0{T.cnt['te']}", [P, D], F32))
            pmW1 = ces.enter_context(nc.psum_tensor(f"pmW1{T.cnt['te']}", [P, D], F32))
            pmWs = [pmW0, pmW1]

            def syw(sy_):
                T.wait_all(sy_, "sy", snap0)
                for c in range(KC):
                    T.dma(sy_, "sy", "dls", out=woS[:, c * D:(c + 1) * D],
                                        in_=wo_ap[c * P:(c + 1) * P, :])
                if has_bias:
                    T.dma(sy_, "sy", "dls", out=borow[:], in_=w["bo"])
            blk.sync(syw)
            wo_loaded = T.cnt["dls"]
            if not has_bias:
                def vz(ve):
                    T.wait_all(ve, "ve", snap0)
                    T.sw(ve, "ve")
                    T.inc(ve.memset(borow[:], 0.0), "ve")
                blk.vector(vz)
            bias_ready = T.cnt["ve"]

            tpw_t, ctT_t, mmw_t, add_t = [], [], [], []
            for i in range(TT):
                def tw(te, i=i):
                    if i == 0:
                        T.wait_all(te, "te", snap0)
                    if i >= 1:
                        T.wait(te, "te", "ve", ctT_t[i - 1])
                    for c in range(KC):
                        ins = te.transpose(psT[:, c * P:(c + 1) * P],
                                           ctx[:, i * D + c * P:i * D + (c + 1) * P],
                                           ident_f[:])
                        if c == KC - 1:
                            tpw_t.append(T.inc(ins, "te"))
                blk.tensor(tw)

                def vw(ve, i=i):
                    T.wait(ve, "ve", "te", tpw_t[i])
                    for nb in range(2):
                        T.sw(ve, "ve")
                        ins = T.inc(ve.tensor_copy(ctT[:, nb * 512:(nb + 1) * 512],
                                                   psT[:, nb * 512:(nb + 1) * 512]),
                                    "ve")
                    ctT_t.append(ins)
                blk.vector(vw)

                def tw2(te, i=i):
                    T.wait(te, "te", "ve", ctT_t[i])
                    T.wait(te, "te", "dls", wo_loaded)
                    T.wait(te, "te", "ve", bias_ready)
                    if i >= 2:
                        T.wait(te, "te", "ve", add_t[i - 2])
                    pm = pmWs[i % 2]
                    for nb in range(2):
                        for c in range(KC):
                            te.matmul(pm[:, nb * 512:(nb + 1) * 512],
                                      ctT[:, c * P:(c + 1) * P],
                                      woS[:, c * D + nb * 512:c * D + (nb + 1) * 512],
                                      start=(c == 0), stop=False)
                        ins = te.matmul(pm[:, nb * 512:(nb + 1) * 512], onesrow_f[:],
                                        borow[0:1, nb * 512:(nb + 1) * 512],
                                        start=False, stop=True)
                        if nb == 1:
                            mmw_t.append(T.inc(ins, "te"))
                blk.tensor(tw2)

                def vw2(ve, i=i):
                    T.wait(ve, "ve", "te", mmw_t[i])
                    for nb in range(2):
                        T.sw(ve, "ve")
                        ins = T.inc(ve.tensor_tensor(
                            xres[:, i * D + nb * 512:i * D + (nb + 1) * 512],
                            pmWs[i % 2][:, nb * 512:(nb + 1) * 512],
                            xres[:, i * D + nb * 512:i * D + (nb + 1) * 512], ALU.add),
                            "ve")
                    add_t.append(ins)
                blk.vector(vw2)

    def attn_l0():
        snap0 = T.snap()
        with contextlib.ExitStack() as actx:
            with contextlib.ExitStack() as ces:
                xlnT = attn_lnT(snap0, ces)
                pj_snap = T.snap()
                wp0 = ces.enter_context(nc.sbuf_tensor("wp0L0", [P, KC * P], F32))
                wp1 = ces.enter_context(nc.sbuf_tensor("wp1L0", [P, KC * P], F32))
                pout0 = ces.enter_context(nc.sbuf_tensor("pout0L0", [P, S], F32))
                pout1 = ces.enter_context(nc.sbuf_tensor("pout1L0", [P, S], F32))
                psqs = [ces.enter_context(nc.sbuf_tensor(f"psq{jj}L0", [P, S], F32))
                        for jj in range(2)]
                vstag = [ces.enter_context(nc.sbuf_tensor(f"vstg{h}L0", [P, TT * 65],
                                                          F32)) for h in range(2)]
                bqp = ces.enter_context(nc.sbuf_tensor("bqpL0", [P, 3 * NPAIR], F32))
                pmL = ces.enter_context(nc.psum_tensor("pmLL0", [P, S], F32))
                pmTr = ces.enter_context(nc.psum_tensor("pmTrL0", [P, D], F32))
                prowD = ces.enter_context(nc.psum_tensor("prowDL0", [2, D], F32))
                nrm2 = ces.enter_context(nc.sbuf_tensor("nrm2L0", [2, 512], F32))
                wps = [wp0, wp1]

                def vz0(ve):
                    T.wait_all(ve, "ve", snap0)
                    T.sw(ve, "ve")
                    T.inc(ve.memset(stats[0:2, SC3:SC4 + 1], 0.0), "ve")
                    for h in range(2):
                        for st_ in range(TT):
                            T.inc(ve.memset(
                                vstag[h][:, st_ * 65 + DK:st_ * 65 + DK + 1], 1.0), "ve")
                blk.vector(vz0)

                def syb(sy_):
                    T.wait_all(sy_, "sy", snap0)
                    T.dma(sy_, "sy", "dls", out=bqp[:, 0:NPAIR], in_=w["bq"])
                    T.dma(sy_, "sy", "dls", out=bqp[:, NPAIR:2 * NPAIR], in_=w["bk"])
                    T.dma(sy_, "sy", "dls", out=bqp[:, 2 * NPAIR:3 * NPAIR], in_=w["bv"])
                blk.sync(syb)
                b_loaded = T.cnt["dls"]

                pouts = [pout0, pout1]
                vz0_ve = T.cnt["ve"]
                mm_tick, ev_tick, pout_free = [], [], []
                wl_t = []
                norm_te_j = [0, 0]
                pending_norm = None
                vspl_byp = {}
                proj_snap = T.snap()
                for p in range(NPAIR):
                    for j, nm in enumerate(("wq", "wk", "wv")):
                        it = p * 3 + j

                        def syw(sy_, p=p, nm=nm, it=it):
                            if it < 2:
                                T.wait_all(sy_, "sy", pj_snap)
                            if it >= 2:
                                T.wait(sy_, "sy", "te", mm_tick[it - 2][3])
                            T.dma(sy_, "sy", "dls", out=wps[it % 2][:], in_=w[nm][p])
                            wl_t.append(T.cnt["dls"])
                        blk.sync(syw)

                        def tp(te, it=it):
                            T.wait(te, "te", "dls", wl_t[it])
                            if it == 0:
                                T.wait_all(te, "te", proj_snap)
                            ticks = []
                            for nb in range(4):
                                if it >= 1:
                                    T.wait(te, "te", "se", ev_tick[it - 1][nb])
                                for k in range(KC):
                                    ins = te.matmul(
                                        pmL[:, nb * 512:(nb + 1) * 512],
                                        wps[it % 2][:, k * P:(k + 1) * P],
                                        xlnT[:, k * S + nb * 512:k * S + (nb + 1) * 512],
                                        start=(k == 0), stop=(k == KC - 1))
                                ticks.append(T.inc(ins, "te"))
                            mm_tick.append(ticks)
                        blk.tensor(tp)

                        def sp(se, p=p, j=j, it=it):
                            T.wait(se, "se", "dls", b_loaded)
                            if it >= 2:
                                for q_, t_ in pout_free[it - 2]:
                                    T.wait(se, "se", q_, t_)
                            ticks = []
                            for nb in range(4):
                                T.wait(se, "se", "te", mm_tick[it][nb])
                                T.sw(se, "se")
                                ticks.append(T.inc(se.activation(
                                    pouts[it % 2][:, nb * 512:(nb + 1) * 512],
                                    pmL[:, nb * 512:(nb + 1) * 512], AF.Identity,
                                    bias=bqp[:, j * NPAIR + p:j * NPAIR + p + 1]),
                                    "se"))
                            ev_tick.append(ticks)
                        blk.scalar(sp)

                        if j < 2:
                            vsq_last = []

                            def vsq(ve, j=j, it=it, nt=norm_te_j[j]):
                                T.wait(ve, "ve", "te", nt)
                                last = None
                                for nb in range(4):
                                    T.wait(ve, "ve", "se", ev_tick[it][nb])
                                    T.sw(ve, "ve")
                                    last = T.inc(ve.tensor_tensor(
                                        psqs[j][:, nb * 512:(nb + 1) * 512],
                                        pouts[it % 2][:, nb * 512:(nb + 1) * 512],
                                        pouts[it % 2][:, nb * 512:(nb + 1) * 512],
                                        ALU.mult), "ve")
                                vsq_last.append(last)
                            blk.vector(vsq)

                            def gsp(ge, p=p, j=j, it=it):
                                T.wait(ge, "ge", "se", ev_tick[it][3])
                                T.dma(ge, "ge", "dsg",
                                      out=(g_q[p] if j == 0 else g_k[p]),
                                      in_=pouts[it % 2][:])
                            blk.gpsimd(gsp)
                            pout_free.append([("ve", vsq_last[0]),
                                              ("dsg", T.cnt["dsg"])])
                            if pending_norm is not None:
                                pj, ptick = pending_norm
                                emit_norm(psqs[pj], prowD, nrm2,
                                          SC3 if pj == 0 else SC4, ("ve", ptick))
                                norm_te_j[pj] = T.cnt["te"]
                            pending_norm = (j, vsq_last[0])
                        else:
                            vtp_loc, vcp_loc = [], []
                            vgate = vspl_byp.get(p - 1)
                            for st_ in range(TT):
                                def tvi(te, st_=st_, it=it):
                                    T.wait(te, "te", "se", ev_tick[it][st_ // 4])
                                    if st_ >= 2:
                                        T.wait(te, "te", "se", vcp_loc[st_ - 2])
                                    vtp_loc.append(
                                        T.inc(te.transpose(
                                            pmTr[:, (st_ % 2) * 512:
                                                 (st_ % 2) * 512 + P],
                                            pouts[it % 2][:, st_ * P:(st_ + 1) * P],
                                            ident_f[:]), "te"))
                                blk.tensor(tvi)

                                def svi(se, st_=st_, vgate=vgate):
                                    T.wait(se, "se", "te", vtp_loc[st_])
                                    if st_ == 0 and vgate is not None:
                                        T.wait(se, "se", "dsg", vgate)
                                    T.sw(se, "se")
                                    se.copy(vstag[0][:, st_ * 65:st_ * 65 + DK],
                                            pmTr[:, (st_ % 2) * 512:
                                                 (st_ % 2) * 512 + DK])
                                    vcp_loc.append(T.inc(
                                        se.copy(vstag[1][:, st_ * 65:st_ * 65 + DK],
                                                pmTr[:, (st_ % 2) * 512 + DK:
                                                     (st_ % 2) * 512 + P]), "se"))
                                blk.scalar(svi)

                            def gspv(ge, p=p):
                                T.wait(ge, "ge", "se", vcp_loc[TT - 1])
                                if p == 0:
                                    T.wait(ge, "ge", "ve", vz0_ve)
                                T.dma(ge, "ge", "dsg", out=g_v[2 * p],
                                      in_=vstag[0][:])
                                vspl_byp[p] = T.dma(ge, "ge", "dsg",
                                                    out=g_v[2 * p + 1],
                                                    in_=vstag[1][:])
                            blk.gpsimd(gspv)
                            pout_free.append([("te", vtp_loc[TT - 1])])
                if pending_norm is not None:
                    pj, ptick = pending_norm
                    emit_norm(psqs[pj], prowD, nrm2,
                              SC3 if pj == 0 else SC4, ("ve", ptick))
                emit_negM()
                emit_negM_bcast(pmTr)

            ctx = actx.enter_context(nc.sbuf_tensor("ctxL0", [P, TT * D], F32))
            scS = actx.enter_context(contextlib.ExitStack())
            vfA = [scS.enter_context(nc.sbuf_tensor(f"vfA{h}", [P, TT * 65], F32))
                   for h in range(2)]
            vfB = [scS.enter_context(nc.sbuf_tensor(f"vfB{h}", [P, TT * 65], F32))
                   for h in range(2)]
            kb0 = scS.enter_context(nc.sbuf_tensor("kb0L0", [P, S], F32))
            kb1 = scS.enter_context(nc.sbuf_tensor("kb1L0", [P, S], F32))
            sc_snap = T.snap()

            def vload(sy_, p, _qb):
                T.dma(sy_, "sy", "dls", out=_qb[:], in_=g_q[p])
                kb = [kb0, kb1][p % 2]
                T.dma(sy_, "sy", "dls", out=kb[:], in_=g_k[p])
                for h in range(2):
                    vf = [vfA, vfB][p % 2][h]
                    T.dma(sy_, "sy", "dls", out=vf[:], in_=g_v[2 * p + h])

            emit_sc(0, ctx,
                    kb_getter=lambda p: [kb0, kb1][p % 2],
                    vf_getter=lambda p: [vfA, vfB][p % 2],
                    load_pair=vload)

            scS.close()
            emit_wo(ctx, w["wo"], has_bias=True)

    def attn_l1():
        snap0 = T.snap()
        with contextlib.ExitStack() as actx:
            kdup = actx.enter_context(nc.sbuf_tensor("kdupL1", [P, S], F32))
            vfL1 = actx.enter_context(nc.sbuf_tensor("vfL1", [P, TT * 65], F32))
            latS = actx.enter_context(contextlib.ExitStack())
            latT = latS.enter_context(nc.sbuf_tensor("latTL1", [P, 4 * S], F32))

            with contextlib.ExitStack() as s1:
                xlnT = attn_lnT(snap0, s1)
                pj_snap1 = T.snap()
                wcd = s1.enter_context(nc.sbuf_tensor("wcdL1", [P, KC * 512], F32))
                pmL = s1.enter_context(nc.psum_tensor("pmL1a", [P, S], F32))

                def syw(sy_):
                    T.wait_all(sy_, "sy", pj_snap1)
                    for k in range(KC):
                        T.dma(sy_, "sy", "dls", out=wcd[:, k * 512:(k + 1) * 512],
                                            in_=w["wcdown"][k * P:(k + 1) * P, :])
                blk.sync(syw)
                wcd_loaded = T.cnt["dls"]
                lat_se = []
                lat_snap = T.snap()
                for m in range(4):
                    def tl(te, m=m):
                        T.wait(te, "te", "dls", wcd_loaded)
                        if m == 0:
                            T.wait_all(te, "te", lat_snap)
                        if m >= 1:
                            T.wait(te, "te", "se", lat_se[m - 1])
                        for nb in range(4):
                            for k in range(KC):
                                ins = te.matmul(
                                    pmL[:, nb * 512:(nb + 1) * 512],
                                    wcd[:, k * 512 + m * P:k * 512 + (m + 1) * P],
                                    xlnT[:, k * S + nb * 512:k * S + (nb + 1) * 512],
                                    start=(k == 0), stop=(k == KC - 1))
                        T.inc(ins, "te")
                    blk.tensor(tl)

                    def sl(se, m=m):
                        T.wait(se, "se", "te", T.cnt["te"])
                        for nb in range(4):
                            T.sw(se, "se")
                            ins = T.inc(se.copy(
                                latT[:, m * S + nb * 512:m * S + (nb + 1) * 512],
                                pmL[:, nb * 512:(nb + 1) * 512]), "se")
                        lat_se.append(ins)
                    blk.scalar(sl)

            with contextlib.ExitStack() as s2:
                wqu = s2.enter_context(nc.sbuf_tensor("wquL1", [P, 4 * D], F32))
                wkv = s2.enter_context(nc.sbuf_tensor("wkvL1", [P, 512], F32))
                pout = s2.enter_context(nc.sbuf_tensor("poutL1", [P, S], F32))
                psqs1 = [s2.enter_context(nc.sbuf_tensor(f"psq{jj}L1", [P, S], F32))
                         for jj in range(2)]
                kvT = s2.enter_context(nc.sbuf_tensor("kvTL1", [P, S], F32))
                vlow = s2.enter_context(nc.sbuf_tensor("vlowL1", [P, S], F32))
                pmL = s2.enter_context(nc.psum_tensor("pmL1b", [P, S], F32))
                pmTr = s2.enter_context(nc.psum_tensor("pmTr1", [P, D], F32))
                prowD = s2.enter_context(nc.psum_tensor("prowD1", [2, D], F32))
                nrm2 = s2.enter_context(nc.sbuf_tensor("nrm2L1", [2, 512], F32))
                s2_snap = T.snap()

                def vz0(ve):
                    T.wait_all(ve, "ve", s2_snap)
                    T.sw(ve, "ve")
                    T.inc(ve.memset(stats[0:2, SC3:SC4 + 1], 0.0), "ve")
                    ap = vfL1[:].rearrange("p (s f) -> p s f", f=65)[:, :, DK:65]
                    T.sw(ve, "ve")
                    T.inc(ve.memset(ap, 1.0), "ve")
                blk.vector(vz0)

                def syw2(sy_):
                    T.wait_all(sy_, "sy", s2_snap)
                    for lk in range(4):
                        T.dma(sy_, "sy", "dls", out=wqu[:, lk * D:(lk + 1) * D],
                                            in_=w["wqup"][lk * P:(lk + 1) * P, :])
                    for lk in range(4):
                        T.dma(sy_, "sy", "dls", out=wkv[:, lk * P:(lk + 1) * P],
                                            in_=w["wkvup"][lk * P:(lk + 1) * P, :])
                blk.sync(syw2)
                wq_loaded = T.cnt["dls"]

                ev_t, spl_t, sqse_t = [], [], []
                norm_te1 = [0, 0]
                pending1 = None
                for p in range(NPAIR):
                    def tq(te, p=p):
                        T.wait(te, "te", "dls", wq_loaded)
                        if p == 0:
                            T.wait_all(te, "te", s2_snap)
                        if p >= 1:
                            T.wait(te, "te", "se", ev_t[p - 1])
                        for nb in range(4):
                            for lk in range(4):
                                ins = te.matmul(
                                    pmL[:, nb * 512:(nb + 1) * 512],
                                    wqu[:, lk * D + p * P:lk * D + (p + 1) * P],
                                    latT[:, lk * S + nb * 512:lk * S + (nb + 1) * 512],
                                    start=(lk == 0), stop=(lk == 3))
                        T.inc(ins, "te")
                    blk.tensor(tq)

                    def sq(se, p=p, nt=norm_te1[p % 2]):
                        T.wait(se, "se", "te", T.cnt["te"])
                        if p >= 1:
                            T.wait(se, "se", "dss", spl_t[p - 1])
                        for nb in range(4):
                            T.sw(se, "se")
                            ins = T.inc(se.copy(pout[:, nb * 512:(nb + 1) * 512],
                                                pmL[:, nb * 512:(nb + 1) * 512]), "se")
                        ev_t.append(ins)
                        T.wait(se, "se", "te", nt)
                        T.sw(se, "se")
                        sqse_t.append(T.inc(se.activation(psqs1[p % 2][:], pout[:],
                                                          AF.Square), "se"))
                        T.sw(se, "se")
                        spl_t.append(T.dma(se, "se", "dss", out=g_q[p], in_=pout[:]))
                    blk.scalar(sq)
                    if pending1 is not None:
                        pp, ptick = pending1
                        emit_norm(psqs1[pp % 2], prowD, nrm2, SC3, ("se", ptick))
                        norm_te1[pp % 2] = T.cnt["te"]
                    pending1 = (p, sqse_t[-1])

                def tkv(te):
                    T.wait(te, "te", "se", ev_t[NPAIR - 1])
                    T.sw(te, "te")
                    for nb in range(4):
                        for lk in range(4):
                            ins = te.matmul(
                                pmL[:, nb * 512:(nb + 1) * 512],
                                wkv[:, lk * P:(lk + 1) * P],
                                latT[:, lk * S + nb * 512:lk * S + (nb + 1) * 512],
                                start=(lk == 0), stop=(lk == 3))
                    T.inc(ins, "te")
                blk.tensor(tkv)

                if pending1 is not None:
                    pp, ptick = pending1
                    emit_norm(psqs1[pp % 2], prowD, nrm2, SC3, ("se", ptick))
                    norm_te1[pp % 2] = T.cnt["te"]

                def skv(se, nt=norm_te1[0]):
                    T.wait(se, "se", "te", T.cnt["te"])
                    T.wait(se, "se", "dss", spl_t[NPAIR - 1])
                    for nb in range(4):
                        T.sw(se, "se")
                        T.inc(se.copy(kvT[:, nb * 512:(nb + 1) * 512],
                                      pmL[:, nb * 512:(nb + 1) * 512]), "se")
                    T.wait(se, "se", "te", nt)
                    T.sw(se, "se")
                    T.inc(se.activation(psqs1[0][:], kvT[:], AF.Square), "se")
                blk.scalar(skv)
                emit_norm(psqs1[0], prowD, nrm2, SC4, ("se", T.cnt["se"]))

                def sykd(sy_):
                    T.wait(sy_, "sy", "se", T.cnt["se"])
                    T.dma(sy_, "sy", "dls", out=kdup[0:DK, :], in_=kvT[0:DK, :])
                    T.dma(sy_, "sy", "dls", out=kdup[DK:P, :], in_=kvT[0:DK, :])
                    T.dma(sy_, "sy", "dls", out=vlow[0:DK, :], in_=kvT[DK:P, :])
                blk.sync(sykd)
                kv_dma = T.cnt["dls"]

                vtp_t, vcp_t1 = [], []
                for st_ in range(TT):
                    def tv(te, st_=st_):
                        if st_ == 0:
                            T.wait(te, "te", "se", T.cnt["se"])
                            T.wait(te, "te", "dls", kv_dma)
                        if st_ >= 2:
                            T.wait(te, "te", "se", vcp_t1[st_ - 2])
                        vtp_t.append(
                            T.inc(te.transpose(
                                pmTr[:, (st_ % 2) * 512:(st_ % 2) * 512 + DK],
                                vlow[0:DK, st_ * P:(st_ + 1) * P],
                                ident_f[0:DK, 0:DK]), "te"))
                    blk.tensor(tv)

                    def sv(se, st_=st_):
                        T.wait(se, "se", "te", vtp_t[st_])
                        T.sw(se, "se")
                        vcp_t1.append(
                            T.inc(se.copy(vfL1[:, st_ * 65:st_ * 65 + DK],
                                          pmTr[:, (st_ % 2) * 512:
                                               (st_ % 2) * 512 + DK]), "se"))
                    blk.scalar(sv)

                emit_negM()
                emit_negM_bcast(pmTr)

            latS.close()
            ctx = actx.enter_context(nc.sbuf_tensor("ctxL1", [P, TT * D], F32))

            def vload1(sy_, p, _qb):
                T.dma(sy_, "sy", "dls", out=_qb[:], in_=g_q[p])

            emit_sc(1, ctx,
                    kb_getter=lambda p: kdup,
                    vf_getter=lambda p: [vfL1, vfL1],
                    load_pair=vload1)
            emit_wo(ctx, w["wcproj"], has_bias=False)

    consts_and_x()
    with contextlib.ExitStack() as pes0:
        ffn_prep(0, pes0)
    if N_SUBLAYERS >= 1:
        ffn_call(0, "f0a")
    if N_SUBLAYERS >= 2:
        attn_l0()
    if N_SUBLAYERS >= 3:
        ffn_call(0, "f0b")
    if N_SUBLAYERS >= 4:
        with contextlib.ExitStack() as pes1:
            ffn_prep(1, pes1)
        ffn_call(1, "f1a")
    if N_SUBLAYERS >= 5:
        attn_l1()
    if N_SUBLAYERS >= 6:
        ffn_call(1, "f1b")

    snapF = T.snap()

    def sfin(se):
        T.wait_all(se, "se", snapF)
        T.sw(se, "se")
        for i in range(TT):
            T.dma(se, "se", "dss", out=out_h[i * P:(i + 1) * P, :],
                               in_=xres[:, i * D:(i + 1) * D])
    blk.scalar(sfin)

    def syfin(sy_):
        T.wait(sy_, "sy", "dss", T.cnt["dss"])
    blk.sync(syfin)

    T.check()
    es.close()
    return nc


def kernel(**inputs):
    f32 = np.float32

    def c(a):
        return np.ascontiguousarray(np.asarray(a), dtype=f32)

    common = {}
    for L in (0, 1):
        common[f"w1t{L}"] = c(np.asarray(inputs[f"L{L}_ffn_w1"]).T)
        common[f"w2t{L}"] = c(np.asarray(inputs[f"L{L}_ffn_w2"]).T)
        common[f"cw{L}"] = c(np.asarray(inputs[f"L{L}_ffn_conv_w"]).reshape(
            ST, P, 3).transpose(1, 0, 2).reshape(P, ST * 3))
        common[f"cb{L}"] = c(np.asarray(inputs[f"L{L}_ffn_conv_b"]).reshape(ST, P).T)
    for nm, key in (("wq", "L0_attn_Wq"), ("wk", "L0_attn_Wk"), ("wv", "L0_attn_Wv")):
        W = np.asarray(inputs[key])
        common[nm] = c(W.reshape(NPAIR, 2, KC, P, DK).transpose(0, 3, 2, 1, 4)
                       .reshape(NPAIR, P, KC * P))
    for nm, key in (("bq", "L0_attn_bq"), ("bk", "L0_attn_bk"), ("bv", "L0_attn_bv")):
        common[nm] = c(np.asarray(inputs[key]).reshape(NPAIR, P).T)
    common["wo"] = c(inputs["L0_attn_Wo"])
    common["bo"] = c(np.asarray(inputs["L0_attn_bo"]).reshape(1, D))
    common["wcdown"] = c(inputs["L1_attn_Wc_down"])
    common["wqup"] = c(inputs["L1_attn_Wq_up"])
    common["wkvup"] = c(inputs["L1_attn_Wkv_up"])
    common["wcproj"] = c(inputs["L1_attn_Wc_proj"])

    x = np.asarray(inputs["L0_attn_embeddings"], dtype=f32)
    nc = build_program()
    in_maps = [dict(common, x=c(x[b])) for b in range(8)]
    res = run_bass_kernel_spmd(nc, in_maps, core_ids=list(range(8)))
    return np.stack([res.results[b]["out"] for b in range(8)], axis=0)


# version 12
# speedup vs baseline: 1.0337x; 1.0337x over previous
import contextlib
import numpy as np
import concourse.bass as bass
import concourse.mybir as mybir
from concourse.bass_utils import run_bass_kernel_spmd

F32 = mybir.dt.float32
I32 = mybir.dt.int32
BF16 = mybir.dt.bfloat16
AF = mybir.ActivationFunctionType
ALU = mybir.AluOpType
AX = mybir.AxisListType

P, S, D, DH, HDS, DK, LAT = 128, 2048, 1024, 4096, 16, 64, 512
TT, ST, KC, NPAIR = S // P, DH // P, D // P, HDS // 2

RB = -0.0009765625
FFN_STAGE = 3
A1_LEVEL = 3
N_SUBLAYERS = 6

MEAN, VAR, RSTD, XMAX, XS, ALPHA, BETA, SC2 = 0, 16, 32, 48, 64, 80, 96, 112
NW1 = 144
THR1, THR2, NTH1, NTH2, M1C, M2C, NEGM = 192, 193, 194, 195, 196, 197, 198
QA, KA, CA, CB = 200, 208, 216, 217
SC1, SC2_, SC3, SC4 = 220, 221, 222, 223
SCR = 256


class TK:
    def __init__(self, nc):
        self.sem, self.cnt, self.lastw = {}, {}, {}
        for n in ("te", "ve", "se", "ge", "dls", "dss", "dlg", "dsg"):
            self.sem[n] = nc.alloc_semaphore(name="sm_" + n)
            self.cnt[n] = 0
        self.maxwait = {n: 0 for n in self.sem}

    def inc(self, instr, n, amount=1):
        instr.then_inc(self.sem[n], amount)
        self.cnt[n] += amount
        return self.cnt[n]

    def wait(self, eng, ename, n, val):
        if val <= 0:
            return
        self.maxwait[n] = max(self.maxwait[n], val)
        if self.lastw.get((ename, n), -1) >= val:
            return
        eng.wait_ge(self.sem[n], val)
        self.lastw[(ename, n)] = val

    def dma(self, eng, ename, qname, out, in_):
        self.wait(eng, ename, qname, self.cnt[qname])
        return self.inc(eng.dma_start(out=out, in_=in_), qname, 16)

    def sw(self, eng, ename):
        self.wait(eng, ename, ename, self.cnt[ename])

    def snap(self):
        return dict(self.cnt)

    def wait_all(self, eng, ename, snap):
        for n, v in snap.items():
            self.wait(eng, ename, n, v)

    def check(self):
        for n, v in self.maxwait.items():
            assert v <= self.cnt[n], f"wait beyond production: {n} {v} > {self.cnt[n]}"


def build_program():
    nc = bass.Bass("TRN2", target_bir_lowering=False)

    x_h = nc.dram_tensor("x", [S, D], F32, kind="ExternalInput").ap()
    out_h = nc.dram_tensor("out", [S, D], F32, kind="ExternalOutput").ap()
    w = {}
    for L in (0, 1):
        w[f"w1t{L}"] = nc.dram_tensor(f"w1t{L}", [D, DH], F32, kind="ExternalInput").ap()
        w[f"w2t{L}"] = nc.dram_tensor(f"w2t{L}", [DH, D], F32, kind="ExternalInput").ap()
        w[f"cw{L}"] = nc.dram_tensor(f"cw{L}", [P, ST * 3], F32, kind="ExternalInput").ap()
        w[f"cb{L}"] = nc.dram_tensor(f"cb{L}", [P, ST], F32, kind="ExternalInput").ap()
    for nm in ("wq", "wk", "wv"):
        w[nm] = nc.dram_tensor(nm, [NPAIR, P, KC * P], F32, kind="ExternalInput").ap()
    for nm in ("bq", "bk", "bv"):
        w[nm] = nc.dram_tensor(nm, [P, NPAIR], F32, kind="ExternalInput").ap()
    w["wo"] = nc.dram_tensor("wo", [D, D], F32, kind="ExternalInput").ap()
    w["bo"] = nc.dram_tensor("bo", [1, D], F32, kind="ExternalInput").ap()
    w["wcdown"] = nc.dram_tensor("wcdown", [D, LAT], F32, kind="ExternalInput").ap()
    w["wqup"] = nc.dram_tensor("wqup", [LAT, D], F32, kind="ExternalInput").ap()
    w["wkvup"] = nc.dram_tensor("wkvup", [LAT, 2 * DK], F32, kind="ExternalInput").ap()
    w["wcproj"] = nc.dram_tensor("wcproj", [D, D], F32, kind="ExternalInput").ap()

    g_wq = nc.dram_tensor("g_wq", [2, P, 32768], BF16, kind="Internal").ap()
    g_silu = nc.dram_tensor("g_silu", [ST, P, S], F32, kind="Internal").ap()
    g_q = nc.dram_tensor("g_q", [NPAIR, P, S], F32, kind="Internal").ap()
    g_k = nc.dram_tensor("g_k", [NPAIR, P, S], F32, kind="Internal").ap()
    g_v = nc.dram_tensor("g_v", [HDS, P, TT * 65], F32, kind="Internal").ap()

    xres = nc.alloc_sbuf_tensor("xres", [P, TT * D], F32)
    stats = nc.alloc_sbuf_tensor("stats", [P, 352], F32)
    ident_bf = nc.alloc_sbuf_tensor("ident_bf", [P, P], BF16)
    ident_f = nc.alloc_sbuf_tensor("ident_f", [P, P], F32)
    maskt = nc.alloc_sbuf_tensor("maskt", [P, P], F32)
    ones_bf = nc.alloc_sbuf_tensor("ones_bf", [P, 1], BF16)
    ones_f = nc.alloc_sbuf_tensor("ones_f", [P, 1], F32)
    onesrow_f = nc.alloc_sbuf_tensor("onesrow_f", [1, P], F32)
    neg128row = nc.alloc_sbuf_tensor("neg128row", [1, P], F32)
    row2_f = nc.alloc_sbuf_tensor("row2_f", [1, D], F32)
    borow = nc.alloc_sbuf_tensor("borow", [1, D], F32)
    convw_sb = nc.alloc_sbuf_tensor("convw_sb", [P, ST * 3], F32)
    convb_sb = nc.alloc_sbuf_tensor("convb_sb", [P, ST], F32)

    es = contextlib.ExitStack()
    blk = es.enter_context(nc.Block())
    T = TK(nc)

    def consts_and_x():
        def g(ge):
            T.inc(ge.iota(ident_f[:], pattern=[[1, P]], base=0, channel_multiplier=-1,
                          allow_small_or_imprecise_dtypes=True), "ge")
        blk.gpsimd(g)

        def v(ve):
            ve.memset(ones_bf[:], 1.0)
            ve.memset(ones_f[:], 1.0)
            ve.memset(onesrow_f[:], 1.0)
            ve.memset(neg128row[:], -128.0)
            ve.memset(stats[0:64, CA:CA + 1], 1.0)
            ve.memset(stats[64:P, CA:CA + 1], 0.0)
            ve.memset(stats[0:64, CB:CB + 1], 0.0)
            ve.memset(stats[64:P, CB:CB + 1], 1.0)
            T.wait(ve, "ve", "ge", T.cnt["ge"])
            T.sw(ve, "ve")
            T.inc(ve.tensor_single_scalar(ident_bf[:], ident_f[:], 0.0, ALU.is_equal), "ve")
            T.sw(ve, "ve")
            T.inc(ve.tensor_single_scalar(maskt[:], ident_f[:], 0.0, ALU.is_ge), "ve")
            T.sw(ve, "ve")
            T.inc(ve.tensor_single_scalar(ident_f[:], ident_f[:], 0.0, ALU.is_equal), "ve")
        blk.vector(v)

        def sy(sy_):
            for i in range(TT):
                T.dma(sy_, "sy", "dls", out=xres[:, i * D:(i + 1) * D],
                                    in_=x_h[i * P:(i + 1) * P, :])
        blk.sync(sy)

    def ffn_prep(L, ces):
        snap0 = T.snap()
        wbA = ces.enter_context(nc.sbuf_tensor(f"wbA{L}", [P, DH], F32))
        wbB = ces.enter_context(nc.sbuf_tensor(f"wbB{L}", [P, DH], F32))
        wbC = ces.enter_context(nc.sbuf_tensor(f"wbC{L}", [P, DH], F32))
        wng = ces.enter_context(nc.sbuf_tensor(f"wng{L}", [P, DH], F32))
        wsl = ces.enter_context(nc.sbuf_tensor(f"wslP{L}", [P, 32768], BF16))
        pp = ces.enter_context(nc.psum_tensor(f"ppP{L}", [P, 512], F32))
        ppr = ces.enter_context(nc.psum_tensor(f"pprP{L}", [1, D], F32))
        bufs = [wbA, wbB, wbC]
        NTIL = KC + ST

        vred_tick = []
        ld1_tick = {}

        def syl_one(eng, ename, qname, i):
            if i < 2:
                T.wait_all(eng, ename, snap0)
            if i >= 3:
                T.wait(eng, ename, "ve", vred_tick[i - 3])
            src = (w[f"w1t{L}"][i * P:(i + 1) * P, :] if i < KC
                   else w[f"w2t{L}"][(i - KC) * P:(i - KC + 1) * P, :])
            dst = bufs[i % 3][:] if i < KC else bufs[i % 3][:, 0:D]
            ld1_tick[i] = T.dma(eng, ename, qname, out=dst, in_=src)
            if i == NTIL - 1:
                T.dma(eng, ename, qname, out=convw_sb[:], in_=w[f"cw{L}"])
                T.dma(eng, ename, qname, out=convb_sb[:], in_=w[f"cb{L}"])

        def vred_chunk(ve, lo, hi):
            if lo == 0:
                T.wait_all(ve, "ve", snap0)
            for i in range(lo, hi):
                T.wait(ve, "ve", "dls" if i % 2 == 0 else "dlg", ld1_tick[i])
                src = bufs[i % 3][:] if i < KC else bufs[i % 3][:, 0:D]
                T.sw(ve, "ve")
                vred_tick.append(
                    T.inc(ve.tensor_reduce(stats[:, SCR + i:SCR + i + 1], src, AX.X,
                                           ALU.add, apply_absolute_value=True), "ve"))

        for lo in range(0, NTIL, 2):
            hi = min(lo + 2, NTIL)
            blk.sync(lambda sy_, lo=lo: syl_one(sy_, "sy", "dls", lo))
            if lo + 1 < NTIL:
                blk.gpsimd(lambda ge, lo=lo: syl_one(ge, "ge", "dlg", lo + 1))
            blk.vector(lambda ve, lo=lo, hi=hi: vred_chunk(ve, lo, hi))

        def tsum(te):
            T.wait_all(te, "te", snap0)
            T.wait(te, "te", "ve", T.cnt["ve"])
            te.matmul(pp[0:KC, 0:1], stats[:, SCR:SCR + KC], ones_f[:, 0:1],
                      start=True, stop=True)
            T.inc(te.matmul(pp[0:ST, 2:3], stats[:, SCR + KC:SCR + NTIL],
                            ones_f[:, 0:1], start=True, stop=True), "te")
        blk.tensor(tsum)

        def vcp(ve):
            T.wait(ve, "ve", "te", T.cnt["te"])
            T.sw(ve, "ve")
            T.inc(ve.tensor_copy(stats[0:KC, SCR + 44:SCR + 45], pp[0:KC, 0:1]), "ve")
            T.sw(ve, "ve")
            T.inc(ve.tensor_copy(stats[0:ST, SCR + 45:SCR + 46], pp[0:ST, 2:3]), "ve")
        blk.vector(vcp)

        def tsum2(te):
            T.wait(te, "te", "ve", T.cnt["ve"])
            T.sw(te, "te")
            te.matmul(pp[0:1, 40:41], stats[0:KC, SCR + 44:SCR + 45], ones_f[0:KC, 0:1],
                      start=True, stop=True)
            T.inc(te.matmul(pp[0:1, 42:43], stats[0:ST, SCR + 45:SCR + 46],
                            ones_f[0:ST, 0:1], start=True, stop=True), "te")
        blk.tensor(tsum2)

        def vsc(ve):
            T.wait(ve, "ve", "te", T.cnt["te"])
            T.sw(ve, "ve")
            T.inc(ve.tensor_copy(stats[0:1, SC1:SC1 + 1], pp[0:1, 40:41]), "ve")
            T.sw(ve, "ve")
            T.inc(ve.tensor_copy(stats[0:1, SC2_:SC2_ + 1], pp[0:1, 42:43]), "ve")
            T.sw(ve, "ve")
            T.inc(ve.tensor_scalar(stats[0:1, SC1:SC2_ + 1], stats[0:1, SC1:SC2_ + 1],
                                   1.0 / (1 << 22), 1e-5, ALU.mult, ALU.max), "ve")
        blk.vector(vsc)

        def tb(te):
            T.wait(te, "te", "ve", T.cnt["ve"])
            T.sw(te, "te")
            te.matmul(pp[0:P, 44:45], onesrow_f[:], stats[0:1, SC1:SC1 + 1],
                      start=True, stop=True)
            T.inc(te.matmul(pp[0:P, 46:47], onesrow_f[:], stats[0:1, SC2_:SC2_ + 1],
                            start=True, stop=True), "te")
        blk.tensor(tb)

        def vb(ve):
            T.wait(ve, "ve", "te", T.cnt["te"])
            T.sw(ve, "ve")
            T.inc(ve.tensor_scalar_mul(stats[:, THR1:THR1 + 1], pp[:, 44:45], 0.5), "ve")
            T.inc(ve.tensor_scalar_mul(stats[:, THR2:THR2 + 1], pp[:, 46:47], 0.5), "ve")
            T.inc(ve.tensor_scalar_mul(stats[:, NTH1:NTH1 + 1], pp[:, 44:45], -0.5), "ve")
            T.inc(ve.tensor_scalar_mul(stats[:, NTH2:NTH2 + 1], pp[:, 46:47], -0.5), "ve")
            T.inc(ve.tensor_scalar_mul(stats[:, M1C:M1C + 1], pp[:, 44:45], 1.0 / 127.0),
                  "ve")
            T.inc(ve.tensor_scalar_mul(stats[:, M2C:M2C + 1], pp[:, 46:47], 0.5 / 127.0),
                  "ve")
        blk.vector(vb)

        thr_ve = T.cnt["ve"]
        tern_tick = []
        ld2_tick = {}

        def syl2_one(eng, ename, qname, i):
            if i < 3:
                T.wait(eng, ename, "ve", thr_ve)
            if i >= 3:
                T.wait(eng, ename, "ve", tern_tick[i - 3])
            ld2_tick[i] = T.dma(eng, ename, qname, out=bufs[i % 3][:],
                                in_=w[f"w1t{L}"][i * P:(i + 1) * P, :])

        def vt1_chunk(ve, lo, hi):
            for i in range(lo, hi):
                T.wait(ve, "ve", "dls" if i % 2 == 0 else "dlg", ld2_tick[i])
                b = bufs[i % 3]
                T.sw(ve, "ve")
                T.inc(ve.tensor_scalar(wng[:], b[:], stats[:, NTH1:NTH1 + 1], None,
                                       ALU.is_lt), "ve")
                T.sw(ve, "ve")
                tern_tick.append(
                    T.inc(ve.scalar_tensor_tensor(wsl[:, i * DH:(i + 1) * DH], b[:],
                                                  stats[:, THR1:THR1 + 1], wng[:],
                                                  ALU.is_gt, ALU.subtract), "ve"))

        for lo in range(0, KC, 2):
            hi = min(lo + 2, KC)
            blk.sync(lambda sy_, lo=lo: syl2_one(sy_, "sy", "dls", lo))
            blk.gpsimd(lambda ge, lo=lo: syl2_one(ge, "ge", "dlg", lo + 1))
            blk.vector(lambda ve, lo=lo, hi=hi: vt1_chunk(ve, lo, hi))

        def tws(te):
            T.wait(te, "te", "ve", T.cnt["ve"])
            T.sw(te, "te")
            for c in range(ST):
                for k in range(KC):
                    ins = te.matmul(pp[:, 64 + c:65 + c],
                                    wsl[:, k * DH + c * P:k * DH + (c + 1) * P],
                                    ones_bf[:, 0:1], start=(k == 0), stop=(k == KC - 1))
                    if k == KC - 1:
                        T.inc(ins, "te")
        blk.tensor(tws)

        def vws(ve):
            T.wait(ve, "ve", "te", T.cnt["te"])
            T.sw(ve, "ve")
            T.inc(ve.tensor_scalar_mul(stats[:, NW1:NW1 + ST], pp[:, 64:64 + ST], -128.0),
                  "ve")
        blk.vector(vws)

        park_tick = {}
        pk_ve = T.cnt["ve"]

        def sypark_sy(sy_):
            T.wait(sy_, "sy", "ve", pk_ve)
            for i, (hhf, k) in enumerate((h, kk) for h in range(2)
                                         for kk in range(KC)):
                if i % 2 == 0:
                    park_tick[(hhf, k)] = ("dss", T.dma(
                        sy_, "sy", "dss",
                        out=g_wq[0][:, hhf * 16384 + k * 2048:
                                    hhf * 16384 + (k + 1) * 2048],
                        in_=wsl[:, k * DH + hhf * 2048:k * DH + (hhf + 1) * 2048]))
        blk.sync(sypark_sy)

        def sypark_ge(ge):
            T.wait(ge, "ge", "ve", pk_ve)
            for i, (hhf, k) in enumerate((h, kk) for h in range(2)
                                         for kk in range(KC)):
                if i % 2 == 1:
                    park_tick[(hhf, k)] = ("dsg", T.dma(
                        ge, "ge", "dsg",
                        out=g_wq[0][:, hhf * 16384 + k * 2048:
                                    hhf * 16384 + (k + 1) * 2048],
                        in_=wsl[:, k * DH + hhf * 2048:k * DH + (hhf + 1) * 2048]))
        blk.gpsimd(sypark_ge)
        park1 = T.cnt["dss"]
        park1g = T.cnt["dsg"]

        tern2_tick = []
        ld3_tick = {}

        def syl3_one(eng, ename, qname, j):
            if j < 3:
                T.wait(eng, ename, "ve", tern_tick[KC - 1])
            if j >= 3:
                T.wait(eng, ename, "ve", tern2_tick[j - 3])
            ld3_tick[j] = T.dma(eng, ename, qname, out=bufs[j % 3][:, 0:D],
                                in_=w[f"w2t{L}"][j * P:(j + 1) * P, :])

        def vt2_chunk(ve, lo, hi):
            for j in range(lo, hi):
                pq, pt = park_tick[((j % 4) // 2, j // 4)]
                T.wait(ve, "ve", pq, pt)
                T.wait(ve, "ve", "dls" if j % 2 == 0 else "dlg", ld3_tick[j])
                b = bufs[j % 3]
                T.sw(ve, "ve")
                T.inc(ve.tensor_scalar(wng[:, 0:D], b[:, 0:D], stats[:, NTH2:NTH2 + 1],
                                       None, ALU.is_lt), "ve")
                T.sw(ve, "ve")
                tern2_tick.append(
                    T.inc(ve.scalar_tensor_tensor(wsl[:, j * D:(j + 1) * D], b[:, 0:D],
                                                  stats[:, THR2:THR2 + 1], wng[:, 0:D],
                                                  ALU.is_gt, ALU.subtract), "ve"))

        for lo in range(0, ST, 2):
            hi = min(lo + 2, ST)
            blk.sync(lambda sy_, lo=lo: syl3_one(sy_, "sy", "dls", lo))
            blk.gpsimd(lambda ge, lo=lo: syl3_one(ge, "ge", "dlg", lo + 1))
            blk.vector(lambda ve, lo=lo, hi=hi: vt2_chunk(ve, lo, hi))

        def tws2(te):
            T.wait(te, "te", "ve", T.cnt["ve"])
            T.sw(te, "te")
            for nb in range(2):
                for c in range(ST):
                    ins = te.matmul(ppr[0:1, nb * 512:(nb + 1) * 512], ones_bf[:, 0:1],
                                    wsl[:, c * D + nb * 512:c * D + (nb + 1) * 512],
                                    start=(c == 0), stop=(c == ST - 1))
                    if c == ST - 1:
                        T.inc(ins, "te")
        blk.tensor(tws2)

        def vws2(ve):
            T.wait(ve, "ve", "te", T.cnt["te"])
            for nb in range(2):
                T.sw(ve, "ve")
                T.inc(ve.tensor_copy(row2_f[0:1, nb * 512:(nb + 1) * 512],
                                     ppr[0:1, nb * 512:(nb + 1) * 512]), "ve")
        blk.vector(vws2)

        pk2_ve = T.cnt["ve"]

        def sypark2(sy_):
            T.wait(sy_, "sy", "ve", pk2_ve)
            for hh in (0, 2):
                T.dma(sy_, "sy", "dss", out=g_wq[1][:, hh * 8192:(hh + 1) * 8192],
                                    in_=wsl[:, hh * 8192:(hh + 1) * 8192])
        blk.sync(sypark2)

        def gepark2(ge):
            T.wait(ge, "ge", "ve", pk2_ve)
            for hh in (1, 3):
                T.dma(ge, "ge", "dsg", out=g_wq[1][:, hh * 8192:(hh + 1) * 8192],
                                    in_=wsl[:, hh * 8192:(hh + 1) * 8192])
        blk.gpsimd(gepark2)

    def ln_stats(snap0, scr):
        def vst(ve):
            T.wait_all(ve, "ve", snap0)
            for i in range(TT):
                T.inc(ve.tensor_reduce(stats[:, MEAN + i:MEAN + i + 1],
                                       xres[:, i * D:(i + 1) * D], AX.X, ALU.add), "ve")
        blk.vector(vst)

        def sst(se):
            T.wait_all(se, "se", snap0)
            for i in range(TT):
                T.sw(se, "se")
                T.inc(se.activation(scr[:], xres[:, i * D:(i + 1) * D], AF.Square,
                                    accum_out=stats[:, VAR + i:VAR + i + 1]), "se")
        blk.scalar(sst)

        def vrs(ve):
            T.wait(ve, "ve", "se", T.cnt["se"])
            T.sw(ve, "ve")
            T.inc(ve.tensor_scalar_mul(stats[:, MEAN:MEAN + TT], stats[:, MEAN:MEAN + TT],
                                       -1.0 / D), "ve")
            T.sw(ve, "ve")
            T.inc(ve.tensor_scalar_mul(stats[:, VAR:VAR + TT], stats[:, VAR:VAR + TT],
                                       1.0 / D), "ve")
            T.sw(ve, "ve")
            T.inc(ve.tensor_tensor(stats[:, SCR:SCR + TT], stats[:, MEAN:MEAN + TT],
                                   stats[:, MEAN:MEAN + TT], ALU.mult), "ve")
            T.sw(ve, "ve")
            T.inc(ve.tensor_tensor(stats[:, VAR:VAR + TT], stats[:, VAR:VAR + TT],
                                   stats[:, SCR:SCR + TT], ALU.subtract), "ve")
            T.sw(ve, "ve")
            T.inc(ve.tensor_single_scalar(stats[:, VAR:VAR + TT], stats[:, VAR:VAR + TT],
                                          1e-5, ALU.add), "ve")
        blk.vector(vrs)

        def srs(se):
            T.wait(se, "se", "ve", T.cnt["ve"])
            T.sw(se, "se")
            T.inc(se.activation(stats[:, RSTD:RSTD + TT], stats[:, VAR:VAR + TT],
                                AF.Sqrt), "se")
        blk.scalar(srs)

        def vnw(ve):
            T.wait(ve, "ve", "se", T.cnt["se"])
            T.sw(ve, "ve")
            T.inc(ve.reciprocal(stats[:, SCR:SCR + TT], stats[:, RSTD:RSTD + TT]), "ve")
            T.sw(ve, "ve")
            T.inc(ve.tensor_tensor(stats[:, SCR + 16:SCR + 32], stats[:, VAR:VAR + TT],
                                   stats[:, SCR:SCR + TT], ALU.mult), "ve")
            T.sw(ve, "ve")
            T.inc(ve.tensor_tensor(stats[:, SCR + 16:SCR + 32],
                                   stats[:, SCR + 16:SCR + 32],
                                   stats[:, SCR:SCR + TT], ALU.mult), "ve")
            T.sw(ve, "ve")
            T.inc(ve.tensor_scalar(stats[:, SCR + 16:SCR + 32],
                                   stats[:, SCR + 16:SCR + 32],
                                   -0.5, 1.5, ALU.mult, ALU.add), "ve")
            T.sw(ve, "ve")
            T.inc(ve.tensor_tensor(stats[:, RSTD:RSTD + TT], stats[:, SCR:SCR + TT],
                                   stats[:, SCR + 16:SCR + 32], ALU.mult), "ve")
        blk.vector(vnw)

    def ffn_call(L, tag):
        snap0 = T.snap()
        with contextlib.ExitStack() as ces:
            aS = ces.enter_context(contextlib.ExitStack())
            xqT = aS.enter_context(nc.sbuf_tensor(f"xqTF{tag}", [P, KC * S], BF16))
            alB = aS.enter_context(nc.sbuf_tensor(f"alBF{tag}", [P, S], F32))
            wslA = aS.enter_context(nc.sbuf_tensor(f"wslA{tag}", [P, 16384], BF16))

            def syw(sy_):
                T.wait_all(sy_, "sy", snap0)
                for hh in range(2):
                    T.dma(sy_, "sy", "dls", out=wslA[:, hh * 8192:(hh + 1) * 8192],
                                        in_=g_wq[0][:, hh * 8192:(hh + 1) * 8192])
            blk.sync(syw)
            w1_loaded = T.cnt["dls"]

            alB_ready = None
            se_cp = []
            with contextlib.ExitStack() as a0:
                xt0 = a0.enter_context(nc.sbuf_tensor(f"xt0{tag}", [P, D], F32))
                xt1 = a0.enter_context(nc.sbuf_tensor(f"xt1{tag}", [P, D], F32))
                r0x = a0.enter_context(nc.sbuf_tensor(f"r0x{tag}", [P, D], I32))
                qxb = a0.enter_context(nc.sbuf_tensor(f"qxb{tag}", [P, D], BF16))
                pmT = a0.enter_context(nc.psum_tensor(f"pmT{tag}", [P, D], BF16))
                prow = a0.enter_context(nc.psum_tensor(f"prow{tag}", [1, S], F32))
                pbc = a0.enter_context(nc.psum_tensor(f"pbc{tag}", [P, D], F32))

                ln_stats(snap0, xt0)

                vq_tick, tp_tick = [], []
                for i in range(TT):
                    def vq(ve, i=i):
                        if i >= 1:
                            T.wait(ve, "ve", "te", tp_tick[i - 1])
                        T.sw(ve, "ve")
                        T.inc(ve.tensor_scalar(xt0[:], xres[:, i * D:(i + 1) * D],
                                               stats[:, MEAN + i:MEAN + i + 1],
                                               stats[:, RSTD + i:RSTD + i + 1],
                                               ALU.add, ALU.mult), "ve")
                        T.sw(ve, "ve")
                        T.inc(ve.tensor_reduce(stats[:, XMAX + i:XMAX + i + 1], xt0[:],
                                               AX.X, ALU.max, apply_absolute_value=True),
                              "ve")
                        T.sw(ve, "ve")
                        T.inc(ve.tensor_single_scalar(stats[:, XMAX + i:XMAX + i + 1],
                                                      stats[:, XMAX + i:XMAX + i + 1],
                                                      1e-5, ALU.max), "ve")
                        T.sw(ve, "ve")
                        T.inc(ve.reciprocal(stats[:, SCR + 40:SCR + 41],
                                            stats[:, XMAX + i:XMAX + i + 1]), "ve")
                        T.sw(ve, "ve")
                        T.inc(ve.tensor_scalar_mul(stats[:, XS + i:XS + i + 1],
                                                   stats[:, SCR + 40:SCR + 41], 127.0),
                              "ve")
                        T.sw(ve, "ve")
                        T.inc(ve.tensor_scalar(xt1[:], xt0[:],
                                               stats[:, XS + i:XS + i + 1], 128.5,
                                               ALU.mult, ALU.add), "ve")
                        T.sw(ve, "ve")
                        T.inc(ve.tensor_single_scalar(r0x[:], xt1[:], RB, ALU.add), "ve")
                        T.sw(ve, "ve")
                        T.inc(ve.tensor_tensor(xt0[:], xt1[:], r0x[:], ALU.subtract),
                              "ve")
                        T.sw(ve, "ve")
                        T.inc(ve.tensor_single_scalar(xt0[:], xt0[:], 1.0, ALU.is_ge),
                              "ve")
                        T.sw(ve, "ve")
                        vq_tick.append(
                            T.inc(ve.scalar_tensor_tensor(qxb[:], xt0[:], 1.0, r0x[:],
                                                          ALU.mult, ALU.add), "ve"))
                    blk.vector(vq)

                    def tq(te, i=i):
                        T.wait(te, "te", "ve", vq_tick[i])
                        if i >= 1:
                            T.wait(te, "te", "se", se_cp[i - 1])
                        for k in range(KC):
                            ins = te.transpose(pmT[:, k * P:(k + 1) * P],
                                               qxb[:, k * P:(k + 1) * P], ident_bf[:])
                            if k == KC - 1:
                                tp_tick.append(T.inc(ins, "te"))
                    blk.tensor(tq)

                    def sq(se, i=i):
                        T.wait(se, "se", "te", tp_tick[i])
                        T.sw(se, "se")
                        for k in range(KC):
                            ins = se.copy(xqT[:, k * S + i * P:k * S + (i + 1) * P],
                                          pmT[:, k * P:(k + 1) * P])
                            if k == KC - 1:
                                se_cp.append(T.inc(ins, "se"))
                    blk.scalar(sq)

                def va(ve):
                    T.sw(ve, "ve")
                    T.inc(ve.tensor_scalar(stats[:, ALPHA:ALPHA + TT],
                                           stats[:, XMAX:XMAX + TT],
                                           stats[:, M1C:M1C + 1], None, ALU.mult), "ve")
                blk.vector(va)
                a_ve = T.cnt["ve"]

                def ta(te):
                    T.wait(te, "te", "ve", a_ve)
                    T.sw(te, "te")
                    for i in range(TT):
                        ins = te.matmul(prow[0:1, i * P:(i + 1) * P],
                                        stats[:, ALPHA + i:ALPHA + i + 1], ident_f[:],
                                        start=True, stop=True)
                        if i == TT - 1:
                            T.inc(ins, "te")
                blk.tensor(ta)
                a_te = T.cnt["te"]

                for hh in range(2):
                    def scp(se, hh=hh):
                        T.wait(se, "se", "te", T.cnt["te"])
                        for nb in range(2):
                            T.sw(se, "se")
                            T.inc(se.copy(
                                borow[0:1, nb * 512:(nb + 1) * 512],
                                prow[0:1, hh * D + nb * 512:hh * D + (nb + 1) * 512]),
                                "se")
                    blk.scalar(scp)

                    def tbc(te, hh=hh):
                        T.wait(te, "te", "se", T.cnt["se"])
                        T.wait(te, "te", "ve", T.cnt["ve"])
                        T.sw(te, "te")
                        te.matmul(pbc[:, 0:512], onesrow_f[:], borow[0:1, 0:512],
                                  start=True, stop=True)
                        T.inc(te.matmul(pbc[:, 512:1024], onesrow_f[:],
                                        borow[0:1, 512:1024], start=True, stop=True),
                              "te")
                    blk.tensor(tbc)

                    def vbc(ve, hh=hh):
                        T.wait(ve, "ve", "te", T.cnt["te"])
                        for nb in range(2):
                            T.sw(ve, "ve")
                            T.inc(ve.tensor_copy(
                                alB[:, hh * D + nb * 512:hh * D + (nb + 1) * 512],
                                pbc[:, nb * 512:(nb + 1) * 512]), "ve")
                    blk.vector(vbc)
                alB_ready = T.cnt["ve"]

            if FFN_STAGE < 2:
                return
            snapA = T.snap()
            with contextlib.ExitStack() as a1:
                sA = a1.enter_context(nc.sbuf_tensor(f"sA{tag}", [P, S], F32))
                sZ0 = a1.enter_context(nc.sbuf_tensor(f"sZ0{tag}", [P, S], F32))
                sZ1 = a1.enter_context(nc.sbuf_tensor(f"sZ1{tag}", [P, S], F32))
                acc = a1.enter_context(nc.sbuf_tensor(f"acc{tag}", [P, S], F32))
                zsq0 = a1.enter_context(nc.sbuf_tensor(f"zsq0{tag}", [P, S], F32))
                zsq1 = a1.enter_context(nc.sbuf_tensor(f"zsq1{tag}", [P, S], F32))
                pmA = a1.enter_context(nc.psum_tensor(f"pmA{tag}", [P, S], F32))
                pmB = a1.enter_context(nc.psum_tensor(f"pmB{tag}", [P, S], F32))
                pms = [pmA, pmB]
                sZ = [sZ0, sZ1]
                zsq = [zsq0, zsq1]

                def vinit(ve):
                    T.wait_all(ve, "ve", snapA)
                    T.sw(ve, "ve")
                    T.inc(ve.memset(acc[:], 0.0), "ve")
                blk.vector(vinit)
                acc_init = T.cnt["ve"]
                ge_snapA = T.snap()

                mm_t, ev_t, base_t, sh_t, silu_t, dss_t = [], [], [], [], [], []
                gq_t, mx_t = [], []
                half_loaded = [w1_loaded, None]
                for c in range(ST):
                    if c == 16:
                        def syw1b(sy_):
                            T.wait(sy_, "sy", "te", mm_t[15])
                            for hh in range(2):
                                T.dma(sy_, "sy", "dls", out=wslA[:, hh * 8192:(hh + 1) * 8192],
                                    in_=g_wq[0][:, 16384 + hh * 8192:
                                                16384 + (hh + 1) * 8192])
                        blk.sync(syw1b)
                        half_loaded[1] = T.cnt["dls"]

                    def tA(te, c=c):
                        if c == 0:
                            T.wait_all(te, "te", snapA)
                        T.wait(te, "te", "dls", half_loaded[c // 16])
                        if c >= 2:
                            T.wait(te, "te", "ve", ev_t[c - 2])
                        pm = pms[c % 2]
                        cc = c % 16
                        for nb in range(4):
                            for k in range(KC):
                                ins = te.matmul(
                                    pm[:, nb * 512:(nb + 1) * 512],
                                    wslA[:, k * 2048 + cc * P:k * 2048 + (cc + 1) * P],
                                    xqT[:, k * S + nb * 512:k * S + (nb + 1) * 512],
                                    start=(k == 0), stop=(k == KC - 1))
                        mm_t.append(T.inc(ins, "te"))
                    blk.tensor(tA)

                    def v1(ve, c=c):
                        T.wait(ve, "ve", "te", mm_t[c])
                        for nb in range(4):
                            T.sw(ve, "ve")
                            ins = T.inc(ve.scalar_tensor_tensor(
                                sA[:, nb * 512:(nb + 1) * 512],
                                pms[c % 2][:, nb * 512:(nb + 1) * 512],
                                stats[:, NW1 + c:NW1 + c + 1],
                                alB[:, nb * 512:(nb + 1) * 512], ALU.add, ALU.mult),
                                "ve")
                        ev_t.append(ins)
                        if A1_LEVEL < 1:
                            sh_t.append(ev_t[-1])
                            return
                        z = sZ[c % 2]
                        if c >= 2 and A1_LEVEL >= 2:
                            T.wait(ve, "ve", "se", silu_t[c - 2])
                            T.wait(ve, "ve", "dss", dss_t[c - 2])
                        if c >= 2 and A1_LEVEL >= 3:
                            T.wait(ve, "ve", "ge", gq_t[c - 2])
                        T.sw(ve, "ve")
                        T.inc(ve.tensor_scalar(z[:], sA[:],
                                               convw_sb[:, 3 * c + 1:3 * c + 2],
                                               convb_sb[:, c:c + 1],
                                               ALU.mult, ALU.add), "ve")
                        T.sw(ve, "ve")
                        T.inc(ve.scalar_tensor_tensor(z[:, 1:S], sA[:, 0:S - 1],
                                                      convw_sb[:, 3 * c:3 * c + 1],
                                                      z[:, 1:S], ALU.mult, ALU.add),
                              "ve")
                        T.sw(ve, "ve")
                        sh_t.append(
                            T.inc(ve.scalar_tensor_tensor(z[:, 0:S - 1], sA[:, 1:S],
                                                          convw_sb[:, 3 * c + 2:3 * c + 3],
                                                          z[:, 0:S - 1], ALU.mult,
                                                          ALU.add), "ve"))
                        if c >= 2 and A1_LEVEL >= 3:
                            T.wait(ve, "ve", "ve", acc_init)
                            T.wait(ve, "ve", "ge", gq_t[c - 2])
                            T.sw(ve, "ve")
                            mx_t.append(T.inc(ve.tensor_tensor(
                                acc[:], acc[:], zsq[c % 2][:], ALU.max), "ve"))
                    blk.vector(v1)

                    def s2(se, cc=c - 1):
                        if A1_LEVEL < 2 or cc < 0:
                            return
                        T.wait(se, "se", "ve", sh_t[cc])
                        z = sZ[cc % 2]
                        T.sw(se, "se")
                        silu_t.append(T.inc(se.activation(z[:], z[:], AF.Silu), "se"))
                        T.sw(se, "se")
                        dss_t.append(T.dma(se, "se", "dss", out=g_silu[cc], in_=z[:]))
                    blk.scalar(s2)

                    def gq1(ge, cc=c - 1):
                        if A1_LEVEL < 3 or cc < 0:
                            return
                        if cc < 2:
                            T.wait_all(ge, "ge", ge_snapA)
                        T.wait(ge, "ge", "se", silu_t[cc])
                        if cc >= 2:
                            T.wait(ge, "ge", "ve", mx_t[cc - 2])
                        gq_t.append(T.inc(ge.tensor_tensor(
                            zsq[cc % 2][:], sZ[cc % 2][:], sZ[cc % 2][:], ALU.mult),
                            "ge"))
                    blk.gpsimd(gq1)

                def s2f(se):
                    if A1_LEVEL < 2:
                        return
                    cc = ST - 1
                    T.wait(se, "se", "ve", sh_t[cc])
                    z = sZ[cc % 2]
                    T.sw(se, "se")
                    silu_t.append(T.inc(se.activation(z[:], z[:], AF.Silu), "se"))
                    T.sw(se, "se")
                    dss_t.append(T.dma(se, "se", "dss", out=g_silu[cc], in_=z[:]))
                blk.scalar(s2f)

                def gq1f(ge):
                    if A1_LEVEL < 3:
                        return
                    cc = ST - 1
                    T.wait(ge, "ge", "se", silu_t[cc])
                    T.wait(ge, "ge", "ve", mx_t[cc - 2])
                    gq_t.append(T.inc(ge.tensor_tensor(
                        zsq[cc % 2][:], sZ[cc % 2][:], sZ[cc % 2][:], ALU.mult), "ge"))
                blk.gpsimd(gq1f)

                def vmaxf(ve):
                    if A1_LEVEL < 3:
                        return
                    for cc in (ST - 2, ST - 1):
                        T.wait(ve, "ve", "ge", gq_t[cc])
                        T.sw(ve, "ve")
                        mx_t.append(T.inc(ve.tensor_tensor(
                            acc[:], acc[:], zsq[cc % 2][:], ALU.max), "ve"))
                blk.vector(vmaxf)


                tp2_t, red_t = [], []
                for i in range(TT if A1_LEVEL >= 3 else 0):
                    def tmx(te, i=i):
                        if i == 0:
                            T.wait(te, "te", "ve", mx_t[ST - 1])
                            T.wait(te, "te", "ve", ev_t[ST - 1])
                        if i >= 2:
                            T.wait(te, "te", "ve", red_t[i - 2])
                        tp2_t.append(
                            T.inc(te.transpose(
                                pmA[:, (i % 2) * 512:(i % 2) * 512 + P],
                                acc[:, i * P:(i + 1) * P], ident_f[:]), "te"))
                    blk.tensor(tmx)

                    def vmx(ve, i=i):
                        T.wait(ve, "ve", "te", tp2_t[i])
                        T.sw(ve, "ve")
                        T.inc(ve.tensor_copy(
                            sA[:, 0:P],
                            pmA[:, (i % 2) * 512:(i % 2) * 512 + P]), "ve")
                        T.sw(ve, "ve")
                        red_t.append(
                            T.inc(ve.tensor_reduce(
                                stats[:, SC2 + i:SC2 + i + 1],
                                sA[:, 0:P], AX.X, ALU.max), "ve"))
                    blk.vector(vmx)

                if A1_LEVEL >= 3:
                    def sfsq(se):
                        T.wait(se, "se", "ve", red_t[TT - 1])
                        T.sw(se, "se")
                        T.inc(se.activation(stats[:, SCR + 64:SCR + 64 + TT],
                                            stats[:, SC2:SC2 + TT], AF.Sqrt), "se")
                    blk.scalar(sfsq)
                sq_se = T.cnt["se"]

                def vfin(ve):
                    if A1_LEVEL < 3:
                        return
                    T.wait(ve, "ve", "se", sq_se)
                    T.sw(ve, "ve")
                    T.inc(ve.reciprocal(stats[:, SCR + 80:SCR + 80 + TT],
                                        stats[:, SCR + 64:SCR + 64 + TT]), "ve")
                    T.sw(ve, "ve")
                    T.inc(ve.tensor_tensor(stats[:, SCR + 80:SCR + 80 + TT],
                                           stats[:, SCR + 80:SCR + 80 + TT],
                                           stats[:, SC2:SC2 + TT], ALU.mult), "ve")
                    T.sw(ve, "ve")
                    T.inc(ve.tensor_tensor(stats[:, SCR + 80:SCR + 80 + TT],
                                           stats[:, SCR + 80:SCR + 80 + TT],
                                           stats[:, SCR + 64:SCR + 64 + TT], ALU.add),
                          "ve")
                    T.sw(ve, "ve")
                    T.inc(ve.tensor_scalar(stats[:, SC2:SC2 + TT],
                                           stats[:, SCR + 80:SCR + 80 + TT],
                                           0.5, 1e-5, ALU.mult, ALU.max), "ve")
                    T.sw(ve, "ve")
                    T.inc(ve.tensor_scalar(stats[:, BETA:BETA + TT],
                                           stats[:, SC2:SC2 + TT],
                                           stats[:, M2C:M2C + 1], None, ALU.mult), "ve")
                    T.sw(ve, "ve")
                    T.inc(ve.reciprocal(stats[:, SCR:SCR + TT], stats[:, SC2:SC2 + TT]),
                          "ve")
                    T.sw(ve, "ve")
                    T.inc(ve.tensor_scalar_mul(stats[:, SC2:SC2 + TT],
                                               stats[:, SCR:SCR + TT], 127.0), "ve")
                blk.vector(vfin)

            if FFN_STAGE < 3:
                return
            aS.close()
            snapB = T.snap()
            with contextlib.ExitStack() as b1:
                wsl = b1.enter_context(nc.sbuf_tensor(f"wslB{tag}", [P, 32768], BF16))

                def ssw2(se):
                    T.wait_all(se, "se", snapB)
                    for hh in range(4):
                        T.dma(se, "se", "dss", out=wsl[:, hh * 8192:(hh + 1) * 8192],
                                            in_=g_wq[1][:, hh * 8192:(hh + 1) * 8192])
                blk.scalar(ssw2)
                w2_loaded = T.cnt["dss"]
                sb2 = b1.enter_context(nc.sbuf_tensor(f"sb2{tag}", [P, S], F32))
                with contextlib.ExitStack() as b0:
                    prow2 = b0.enter_context(nc.psum_tensor(f"prw2{tag}", [1, S], F32))
                    pbc2 = b0.enter_context(nc.psum_tensor(f"pbc2{tag}", [P, D], F32))

                    def tr2(te):
                        T.wait_all(te, "te", snapB)
                        T.sw(te, "te")
                        for i in range(TT):
                            ins = te.matmul(prow2[0:1, i * P:(i + 1) * P],
                                            stats[:, SC2 + i:SC2 + i + 1], ident_f[:],
                                            start=True, stop=True)
                            if i == TT - 1:
                                T.inc(ins, "te")
                    blk.tensor(tr2)

                    for hh in range(2):
                        def scp2(se, hh=hh):
                            T.wait_all(se, "se", snapB)
                            T.wait(se, "se", "te", T.cnt["te"])
                            for nb in range(2):
                                T.sw(se, "se")
                                T.inc(se.copy(
                                    borow[0:1, nb * 512:(nb + 1) * 512],
                                    prow2[0:1, hh * D + nb * 512:
                                          hh * D + (nb + 1) * 512]), "se")
                        blk.scalar(scp2)

                        def tbc2(te, hh=hh):
                            T.wait(te, "te", "se", T.cnt["se"])
                            T.wait(te, "te", "ve", T.cnt["ve"])
                            T.sw(te, "te")
                            te.matmul(pbc2[:, 0:512], onesrow_f[:], borow[0:1, 0:512],
                                      start=True, stop=True)
                            T.inc(te.matmul(pbc2[:, 512:1024], onesrow_f[:],
                                            borow[0:1, 512:1024], start=True, stop=True),
                                  "te")
                        blk.tensor(tbc2)

                        def vbc2(ve, hh=hh):
                            T.wait_all(ve, "ve", snapB)
                            T.wait(ve, "ve", "te", T.cnt["te"])
                            for nb in range(2):
                                T.sw(ve, "ve")
                                T.inc(ve.tensor_copy(
                                    sb2[:, hh * D + nb * 512:hh * D + (nb + 1) * 512],
                                    pbc2[:, nb * 512:(nb + 1) * 512]), "ve")
                        blk.vector(vbc2)
                sb2_ready = T.cnt["ve"]

                rls = [b1.enter_context(nc.sbuf_tensor(f"rl{j}{tag}", [P, 1024], F32))
                       for j in range(3)]
                Ubs = [b1.enter_context(nc.sbuf_tensor(f"Ub{j}{tag}", [P, 1024], F32))
                       for j in range(2)]
                r0b = b1.enter_context(nc.sbuf_tensor(f"r0b{tag}", [P, 1024], I32))
                qsl = [b1.enter_context(nc.sbuf_tensor(f"qsl{j}{tag}", [P, 1024], BF16))
                       for j in range(2)]
                pmBq = [b1.enter_context(nc.psum_tensor(f"pmq{j}{tag}", [P, D], F32))
                        for j in range(4)]

                vq_t, mm2_t, res_t, fix_t, gm_t = {}, {}, {}, {}, {}
                NIT = 4 * ST // 2

                def syr(eng, ename, qname, it):
                    tbb, cp = it // (ST // 2), it % (ST // 2)
                    if it < 3:
                        T.wait_all(eng, ename, snapB)
                    if it >= 3:
                        T.wait(eng, ename, "ve", vq_t[it - 3][0])
                    for hh in range(2):
                        T.dma(eng, ename, qname,
                              out=rls[it % 3][:, hh * 512:(hh + 1) * 512],
                              in_=g_silu[2 * cp + hh][:,
                                                      tbb * 512:(tbb + 1) * 512])

                def gmul(ge, it, ld_tick):
                    tbb = it // (ST // 2)
                    if it == 0:
                        T.wait(ge, "ge", "ve", sb2_ready)
                    T.wait(ge, "ge", "dlg" if it % 2 == 0 else "dls", ld_tick)
                    if it >= 2:
                        T.wait(ge, "ge", "ve", vq_t[it - 2][1])
                    for hh in range(2):
                        if hh == 1:
                            T.sw(ge, "ge")
                        ins = ge.tensor_tensor(
                            Ubs[it % 2][:, hh * 512:(hh + 1) * 512],
                            rls[it % 3][:, hh * 512:(hh + 1) * 512],
                            sb2[:, tbb * 512:(tbb + 1) * 512], ALU.mult)
                    gm_t[it] = T.inc(ins, "ge")

                def vqb(ve, it):
                    T.wait(ve, "ve", "ge", gm_t[it])
                    if it >= 2:
                        T.wait(ve, "ve", "te", mm2_t[it - 2])
                    buf = rls[it % 3]
                    Ub = Ubs[it % 2]
                    T.sw(ve, "ve")
                    T.inc(ve.tensor_single_scalar(r0b[:], Ub[:], 128.4990234375,
                                                  ALU.add), "ve")
                    T.sw(ve, "ve")
                    T.inc(ve.scalar_tensor_tensor(
                        buf[:], Ub[:], 128.5, r0b[:], ALU.add, ALU.subtract), "ve")
                    T.sw(ve, "ve")
                    e_tick = T.inc(ve.tensor_single_scalar(
                        Ub[:], buf[:], 1.0, ALU.is_ge), "ve")
                    T.sw(ve, "ve")
                    q_tick = T.inc(ve.scalar_tensor_tensor(
                        qsl[it % 2][:], Ub[:], 1.0, r0b[:], ALU.mult, ALU.add), "ve")
                    vq_t[it] = (e_tick, q_tick)

                def tBc(te, it):
                    tbb, cp = it // (ST // 2), it % (ST // 2)
                    T.wait(te, "te", "ve", vq_t[it][1])
                    if it == 0:
                        T.wait(te, "te", "dss", w2_loaded)
                    if cp == 0 and tbb >= 1:
                        for sub in range(4):
                            T.wait(te, "te", "ve", res_t[(tbb - 1, sub)])
                    last = None
                    for hh in range(2):
                        c = 2 * cp + hh
                        for sub in range(4):
                            for nb in range(2):
                                last = te.matmul(
                                    pmBq[sub][:, nb * 512:(nb + 1) * 512],
                                    qsl[it % 2][:, hh * 512 + sub * P:
                                                hh * 512 + (sub + 1) * P],
                                    wsl[:, c * D + nb * 512:c * D + (nb + 1) * 512],
                                    start=(c == 0), stop=False)
                    mm2_t[it] = T.inc(last, "te")

                def emit_bfix(tbb):
                    for sub in range(4):
                        def tfix(te, tbb=tbb, sub=sub):
                            last = None
                            for nb in range(2):
                                last = te.matmul(
                                    pmBq[sub][:, nb * 512:(nb + 1) * 512],
                                    neg128row[:],
                                    row2_f[0:1, nb * 512:(nb + 1) * 512],
                                    start=False, stop=True)
                            fix_t[(tbb, sub)] = T.inc(last, "te")
                        blk.tensor(tfix)

                        def vres(ve, tbb=tbb, sub=sub):
                            tb = tbb * 4 + sub
                            T.wait(ve, "ve", "te", fix_t[(tbb, sub)])
                            last = None
                            for nb in range(2):
                                T.sw(ve, "ve")
                                last = T.inc(ve.scalar_tensor_tensor(
                                    xres[:, tb * D + nb * 512:tb * D + (nb + 1) * 512],
                                    pmBq[sub][:, nb * 512:(nb + 1) * 512],
                                    stats[:, BETA + tb:BETA + tb + 1],
                                    xres[:, tb * D + nb * 512:tb * D + (nb + 1) * 512],
                                    ALU.mult, ALU.add), "ve")
                            res_t[(tbb, sub)] = last
                        blk.vector(vres)

                ld_ticks = {}
                for g in range(NIT + 3):
                    if g >= 2 and g - 2 < NIT:
                        blk.vector(lambda ve, it=g - 2: vqb(ve, it))
                    if g < NIT:
                        if g % 2 == 0:
                            blk.gpsimd(lambda ge, it=g: syr(ge, "ge", "dlg", it))
                            ld_ticks[g] = T.cnt["dlg"]
                        else:
                            blk.sync(lambda sy_, it=g: syr(sy_, "sy", "dls", it))
                            ld_ticks[g] = T.cnt["dls"]
                    if g >= 1 and g - 1 < NIT:
                        blk.gpsimd(lambda ge, it=g - 1: gmul(ge, it, ld_ticks[g - 1]))
                    if g >= 3:
                        blk.tensor(lambda te, it=g - 3: tBc(te, it))
                        if (g - 3) % (ST // 2) == ST // 2 - 1:
                            emit_bfix((g - 3) // (ST // 2))

    def attn_lnT(snap0, ces):
        xlnT = ces.enter_context(nc.sbuf_tensor(f"xlnT{len(str(snap0))}{T.cnt['te']}",
                                                [P, KC * S], F32))
        with contextlib.ExitStack() as a0:
            xt0a = a0.enter_context(nc.sbuf_tensor(f"xtA{T.cnt['te']}", [P, D], F32))
            xt0b = a0.enter_context(nc.sbuf_tensor(f"xtB{T.cnt['te']}", [P, D], F32))
            pmTf = a0.enter_context(nc.psum_tensor(f"pmTA{T.cnt['te']}", [P, D], F32))
            xts = [xt0a, xt0b]
            ln_stats(snap0, xt0a)
            ln_t, tp_t, cp_t = [], [], []
            for i in range(TT):
                def vln(ve, i=i):
                    if i >= 2:
                        T.wait(ve, "ve", "te", tp_t[i - 2])
                    T.sw(ve, "ve")
                    ln_t.append(
                        T.inc(ve.tensor_scalar(xts[i % 2][:],
                                               xres[:, i * D:(i + 1) * D],
                                               stats[:, MEAN + i:MEAN + i + 1],
                                               stats[:, RSTD + i:RSTD + i + 1],
                                               ALU.add, ALU.mult), "ve"))
                blk.vector(vln)

                def tln(te, i=i):
                    T.wait(te, "te", "ve", ln_t[i])
                    if i >= 1:
                        T.wait(te, "te", "se", cp_t[i - 1])
                    for k in range(KC):
                        ins = te.transpose(pmTf[:, k * P:(k + 1) * P],
                                           xts[i % 2][:, k * P:(k + 1) * P],
                                           ident_f[:])
                        if k == KC - 1:
                            tp_t.append(T.inc(ins, "te"))
                blk.tensor(tln)

                def sln(se, i=i):
                    T.wait(se, "se", "te", tp_t[i])
                    T.sw(se, "se")
                    for k in range(KC):
                        ins = se.copy(xlnT[:, k * S + i * P:k * S + (i + 1) * P],
                                      pmTf[:, k * P:(k + 1) * P])
                        if k == KC - 1:
                            cp_t.append(T.inc(ins, "se"))
                blk.scalar(sln)
        return xlnT

    norm_state = {"prev_ve": 0}

    def emit_norm(psq_ap, prowD, nrm2, run_col, ready):
        for tb4 in range(4):
            def tn2(te, tb4=tb4, rd=ready, pv=norm_state["prev_ve"]):
                T.wait(te, "te", rd[0], rd[1])
                T.wait(te, "te", "ve", pv)
                T.sw(te, "te")
                T.inc(te.matmul(prowD[0:2, 0:512], stats[:, CA:CA + 2],
                                psq_ap[:, tb4 * 512:(tb4 + 1) * 512],
                                start=True, stop=True), "te")
            blk.tensor(tn2)

            def vn2(ve, tb4=tb4):
                T.wait(ve, "ve", "te", T.cnt["te"])
                T.sw(ve, "ve")
                T.inc(ve.tensor_copy(nrm2[0:2, 0:512], prowD[0:2, 0:512]), "ve")
                T.sw(ve, "ve")
                T.inc(ve.tensor_reduce(stats[0:2, SCR + 47:SCR + 48],
                                       nrm2[0:2, 0:512], AX.X, ALU.max), "ve")
                T.sw(ve, "ve")
                T.inc(ve.tensor_tensor(stats[0:2, run_col:run_col + 1],
                                       stats[0:2, run_col:run_col + 1],
                                       stats[0:2, SCR + 47:SCR + 48], ALU.max), "ve")
            blk.vector(vn2)
            norm_state["prev_ve"] = T.cnt["ve"]

    def emit_negM():
        def symrg(sy_):
            T.wait(sy_, "sy", "ve", T.cnt["ve"])
            T.dma(sy_, "sy", "dls", out=stats[0:1, SCR + 90:SCR + 92],
                  in_=stats[1:2, SC3:SC4 + 1])
        blk.sync(symrg)
        mrg_tick = T.cnt["dls"]

        def v1(ve):
            T.wait(ve, "ve", "dls", mrg_tick)
            T.sw(ve, "ve")
            T.inc(ve.tensor_tensor(stats[0:1, SC3:SC4 + 1],
                                   stats[0:1, SC3:SC4 + 1],
                                   stats[0:1, SCR + 90:SCR + 92], ALU.max), "ve")
            T.sw(ve, "ve")
            T.inc(ve.tensor_tensor(stats[0:1, SC1:SC1 + 1], stats[0:1, SC3:SC3 + 1],
                                   stats[0:1, SC4:SC4 + 1], ALU.mult), "ve")
        blk.vector(v1)

        def s1(se):
            T.wait(se, "se", "ve", T.cnt["ve"])
            T.sw(se, "se")
            T.inc(se.activation(stats[0:1, SC1:SC1 + 1], stats[0:1, SC1:SC1 + 1],
                                AF.Sqrt), "se")
        blk.scalar(s1)

        def v2(ve):
            T.wait(ve, "ve", "se", T.cnt["se"])
            T.sw(ve, "ve")
            T.inc(ve.tensor_scalar_mul(stats[0:1, SC1:SC1 + 1], stats[0:1, SC1:SC1 + 1],
                                       -0.125), "ve")
        blk.vector(v2)

    def emit_negM_bcast(pm_small):
        def t1(te):
            T.wait(te, "te", "ve", T.cnt["ve"])
            T.wait(te, "te", "se", T.cnt["se"])
            T.sw(te, "te")
            T.inc(te.matmul(pm_small[0:P, 0:1], onesrow_f[:], stats[0:1, SC1:SC1 + 1],
                            start=True, stop=True), "te")
        blk.tensor(t1)

        def v3(ve):
            T.wait(ve, "ve", "te", T.cnt["te"])
            T.sw(ve, "ve")
            T.inc(ve.tensor_copy(stats[:, NEGM:NEGM + 1], pm_small[0:P, 0:1]), "ve")
        blk.vector(v3)

    def emit_sc(L, ctx, kb_getter, vf_getter, load_pair):
        with contextlib.ExitStack() as ces:
            qb0 = ces.enter_context(nc.sbuf_tensor(f"qb0L{L}", [P, S], F32))
            qb1 = ces.enter_context(nc.sbuf_tensor(f"qb1L{L}", [P, S], F32))
            pr = [[ces.enter_context(nc.sbuf_tensor(f"pr{h}{pp}L{L}", [P, 256], F32))
                   for pp in range(2)] for h in range(2)]
            psS = [ces.enter_context(nc.psum_tensor(f"psS{j}L{L}", [P, 512], F32))
                   for j in range(4)]
            psC = [ces.enter_context(nc.psum_tensor(f"psC{j}L{L}", [P, 512], F32))
                   for j in range(4)]
            qbs = [qb0, qb1]
            qb_tick = [0, 0]
            pair_te_end = [0] * NPAIR
            prev_evac = 0

            sc_snap0 = T.snap()
            for p in range(NPAIR):
                def syq(sy_, p=p):
                    if p < 2:
                        T.wait_all(sy_, "sy", sc_snap0)
                    else:
                        T.wait(sy_, "sy", "te", pair_te_end[p - 2])
                    load_pair(sy_, p, qbs[p % 2])
                blk.sync(syq)
                qb_tick[p % 2] = T.cnt["dls"]
                kb, kb_tick, vf, vf_tick = kb_getter(p), None, vf_getter(p), None

                for qc in range(8):
                    n = 2 * qc + 2
                    te_base = T.cnt["te"]
                    se_base = T.cnt["se"]
                    ve_base = T.cnt["ve"]
                    my_prev_evac = prev_evac
                    my_qb_tick = qb_tick[p % 2]

                    def tsc(te, p=p, qc=qc, n=n, te_base=te_base, se_base=se_base,
                            ve_base=ve_base, my_prev_evac=my_prev_evac,
                            my_qb_tick=my_qb_tick, kb=kb, vf=vf):
                        qb = qbs[p % 2]
                        if p == 0 and qc == 0:
                            T.wait_all(te, "te", sc_snap0)

                        def emit_ctx(j):
                            T.wait(te, "te", "se", se_base + 2 * (j + 1))
                            if j >= 2 * qc:
                                T.wait(te, "te", "ve", ve_base + 2 * (j - 2 * qc + 1))
                            if j == 0:
                                T.wait(te, "te", "ve", my_prev_evac)
                            last = None
                            for qt_local in range(2):
                                qt = 2 * qc + qt_local
                                if qt < j:
                                    continue
                                for hd in range(2):
                                    last = te.matmul(
                                        psC[2 * qt_local + hd][:, 0:65],
                                        pr[hd][j % 2][:, qt_local * P:(qt_local + 1) * P],
                                        vf[hd][:, j * 65:j * 65 + 65],
                                        start=(j == 0), stop=(j == qt))
                            T.inc(last, "te")

                        for v in range(n):
                            if v == 0:
                                T.wait(te, "te", "dls", my_qb_tick)
                                T.wait(te, "te", "se", se_base)
                            if v >= 2:
                                T.wait(te, "te", "se", se_base + 2 * (v - 1))
                            for hd in range(2):
                                ins = te.matmul(
                                    psS[2 * (v % 2) + hd][:, 0:256],
                                    kb[hd * DK:(hd + 1) * DK, v * P:(v + 1) * P],
                                    qb[hd * DK:(hd + 1) * DK, qc * 256:(qc + 1) * 256],
                                    start=True, stop=True, tile_position=(64 * hd, 0))
                                if hd == 1:
                                    T.inc(ins, "te")
                            if v >= 1:
                                emit_ctx(v - 1)
                        emit_ctx(n - 1)
                    blk.tensor(tsc)

                    def ssc(se, p=p, qc=qc, n=n, te_base=te_base):
                        for v in range(n):
                            T.wait(se, "se", "te",
                                   te_base + (1 if v == 0 else 2 * v))
                            T.sw(se, "se")
                            for hd in range(2):
                                T.inc(se.activation(pr[hd][v % 2][:],
                                                    psS[2 * (v % 2) + hd][:, 0:256],
                                                    AF.Exp,
                                                    bias=stats[:, NEGM:NEGM + 1],
                                                    scale=0.125), "se")
                    blk.scalar(ssc)

                    def vsc(ve, p=p, qc=qc, n=n, te_base=te_base, se_base=se_base):
                        for v in (2 * qc, 2 * qc + 1):
                            qt_local = v - 2 * qc
                            T.wait(ve, "ve", "se", se_base + 2 * (v + 1))
                            for hd in range(2):
                                T.sw(ve, "ve")
                                T.inc(ve.tensor_tensor(
                                    pr[hd][v % 2][:, qt_local * P:(qt_local + 1) * P],
                                    pr[hd][v % 2][:, qt_local * P:(qt_local + 1) * P],
                                    maskt[:], ALU.mult), "ve")
                        for qt_local in range(2):
                            qt = 2 * qc + qt_local
                            T.wait(ve, "ve", "te",
                                   te_base + (2 * qt + 3 if qt <= n - 2
                                              else 2 * n))
                            for hd in range(2):
                                slot = psC[2 * qt_local + hd]
                                T.sw(ve, "ve")
                                T.inc(ve.reciprocal(stats[:, SCR + 48:SCR + 49],
                                                    slot[:, DK:DK + 1]), "ve")
                                T.sw(ve, "ve")
                                T.inc(ve.tensor_scalar(
                                    ctx[:, qt * D + p * P + hd * DK:
                                        qt * D + p * P + (hd + 1) * DK],
                                    slot[:, 0:DK], stats[:, SCR + 48:SCR + 49], None,
                                    ALU.mult), "ve")
                    blk.vector(vsc)
                    prev_evac = T.cnt["ve"]
                pair_te_end[p] = T.cnt["te"]

    def emit_wo(ctx, wo_ap, has_bias):
        snap0 = T.snap()
        with contextlib.ExitStack() as ces:
            woS = ces.enter_context(nc.sbuf_tensor(f"woS{T.cnt['te']}", [P, 8192], F32))
            ctT = ces.enter_context(nc.sbuf_tensor(f"ctT{T.cnt['te']}", [P, D], F32))
            psT = ces.enter_context(nc.psum_tensor(f"psT{T.cnt['te']}", [P, D], F32))
            pmW0 = ces.enter_context(nc.psum_tensor(f"pmW0{T.cnt['te']}", [P, D], F32))
            pmW1 = ces.enter_context(nc.psum_tensor(f"pmW1{T.cnt['te']}", [P, D], F32))
            pmWs = [pmW0, pmW1]

            def syw(sy_):
                T.wait_all(sy_, "sy", snap0)
                for c in range(KC):
                    T.dma(sy_, "sy", "dls", out=woS[:, c * D:(c + 1) * D],
                                        in_=wo_ap[c * P:(c + 1) * P, :])
                if has_bias:
                    T.dma(sy_, "sy", "dls", out=borow[:], in_=w["bo"])
            blk.sync(syw)
            wo_loaded = T.cnt["dls"]
            if not has_bias:
                def vz(ve):
                    T.wait_all(ve, "ve", snap0)
                    T.sw(ve, "ve")
                    T.inc(ve.memset(borow[:], 0.0), "ve")
                blk.vector(vz)
            bias_ready = T.cnt["ve"]

            tpw_t, ctT_t, mmw_t, add_t = [], [], [], []
            for i in range(TT):
                def tw(te, i=i):
                    if i == 0:
                        T.wait_all(te, "te", snap0)
                    if i >= 1:
                        T.wait(te, "te", "ve", ctT_t[i - 1])
                    for c in range(KC):
                        ins = te.transpose(psT[:, c * P:(c + 1) * P],
                                           ctx[:, i * D + c * P:i * D + (c + 1) * P],
                                           ident_f[:])
                        if c == KC - 1:
                            tpw_t.append(T.inc(ins, "te"))
                blk.tensor(tw)

                def vw(ve, i=i):
                    T.wait(ve, "ve", "te", tpw_t[i])
                    for nb in range(2):
                        T.sw(ve, "ve")
                        ins = T.inc(ve.tensor_copy(ctT[:, nb * 512:(nb + 1) * 512],
                                                   psT[:, nb * 512:(nb + 1) * 512]),
                                    "ve")
                    ctT_t.append(ins)
                blk.vector(vw)

                def tw2(te, i=i):
                    T.wait(te, "te", "ve", ctT_t[i])
                    T.wait(te, "te", "dls", wo_loaded)
                    T.wait(te, "te", "ve", bias_ready)
                    if i >= 2:
                        T.wait(te, "te", "ve", add_t[i - 2])
                    pm = pmWs[i % 2]
                    for nb in range(2):
                        for c in range(KC):
                            te.matmul(pm[:, nb * 512:(nb + 1) * 512],
                                      ctT[:, c * P:(c + 1) * P],
                                      woS[:, c * D + nb * 512:c * D + (nb + 1) * 512],
                                      start=(c == 0), stop=False)
                        ins = te.matmul(pm[:, nb * 512:(nb + 1) * 512], onesrow_f[:],
                                        borow[0:1, nb * 512:(nb + 1) * 512],
                                        start=False, stop=True)
                        if nb == 1:
                            mmw_t.append(T.inc(ins, "te"))
                blk.tensor(tw2)

                def vw2(ve, i=i):
                    T.wait(ve, "ve", "te", mmw_t[i])
                    for nb in range(2):
                        T.sw(ve, "ve")
                        ins = T.inc(ve.tensor_tensor(
                            xres[:, i * D + nb * 512:i * D + (nb + 1) * 512],
                            pmWs[i % 2][:, nb * 512:(nb + 1) * 512],
                            xres[:, i * D + nb * 512:i * D + (nb + 1) * 512], ALU.add),
                            "ve")
                    add_t.append(ins)
                blk.vector(vw2)

    def attn_l0():
        snap0 = T.snap()
        with contextlib.ExitStack() as actx:
            with contextlib.ExitStack() as ces:
                xlnT = attn_lnT(snap0, ces)
                pj_snap = T.snap()
                wp0 = ces.enter_context(nc.sbuf_tensor("wp0L0", [P, KC * P], F32))
                wp1 = ces.enter_context(nc.sbuf_tensor("wp1L0", [P, KC * P], F32))
                pout0 = ces.enter_context(nc.sbuf_tensor("pout0L0", [P, S], F32))
                pout1 = ces.enter_context(nc.sbuf_tensor("pout1L0", [P, S], F32))
                psqs = [ces.enter_context(nc.sbuf_tensor(f"psq{jj}L0", [P, S], F32))
                        for jj in range(2)]
                vstag = [ces.enter_context(nc.sbuf_tensor(f"vstg{h}L0", [P, TT * 65],
                                                          F32)) for h in range(2)]
                bqp = ces.enter_context(nc.sbuf_tensor("bqpL0", [P, 3 * NPAIR], F32))
                pmL = ces.enter_context(nc.psum_tensor("pmLL0", [P, S], F32))
                pmTr = ces.enter_context(nc.psum_tensor("pmTrL0", [P, D], F32))
                prowD = ces.enter_context(nc.psum_tensor("prowDL0", [2, D], F32))
                nrm2 = ces.enter_context(nc.sbuf_tensor("nrm2L0", [2, 512], F32))
                wps = [wp0, wp1]

                def vz0(ve):
                    T.wait_all(ve, "ve", snap0)
                    T.sw(ve, "ve")
                    T.inc(ve.memset(stats[0:2, SC3:SC4 + 1], 0.0), "ve")
                    for h in range(2):
                        for st_ in range(TT):
                            T.inc(ve.memset(
                                vstag[h][:, st_ * 65 + DK:st_ * 65 + DK + 1], 1.0), "ve")
                blk.vector(vz0)

                def syb(sy_):
                    T.wait_all(sy_, "sy", snap0)
                    T.dma(sy_, "sy", "dls", out=bqp[:, 0:NPAIR], in_=w["bq"])
                    T.dma(sy_, "sy", "dls", out=bqp[:, NPAIR:2 * NPAIR], in_=w["bk"])
                    T.dma(sy_, "sy", "dls", out=bqp[:, 2 * NPAIR:3 * NPAIR], in_=w["bv"])
                blk.sync(syb)
                b_loaded = T.cnt["dls"]

                pouts = [pout0, pout1]
                vz0_ve = T.cnt["ve"]
                mm_tick, ev_tick, pout_free = [], [], []
                wl_t = []
                norm_te_j = [0, 0]
                pending_norm = None
                vspl_byp = {}
                proj_snap = T.snap()
                for p in range(NPAIR):
                    for j, nm in enumerate(("wq", "wk", "wv")):
                        it = p * 3 + j

                        def syw(sy_, p=p, nm=nm, it=it):
                            if it < 2:
                                T.wait_all(sy_, "sy", pj_snap)
                            if it >= 2:
                                T.wait(sy_, "sy", "te", mm_tick[it - 2][3])
                            T.dma(sy_, "sy", "dls", out=wps[it % 2][:], in_=w[nm][p])
                            wl_t.append(T.cnt["dls"])
                        blk.sync(syw)

                        def tp(te, it=it):
                            T.wait(te, "te", "dls", wl_t[it])
                            if it == 0:
                                T.wait_all(te, "te", proj_snap)
                            ticks = []
                            for nb in range(4):
                                if it >= 1:
                                    T.wait(te, "te", "se", ev_tick[it - 1][nb])
                                for k in range(KC):
                                    ins = te.matmul(
                                        pmL[:, nb * 512:(nb + 1) * 512],
                                        wps[it % 2][:, k * P:(k + 1) * P],
                                        xlnT[:, k * S + nb * 512:k * S + (nb + 1) * 512],
                                        start=(k == 0), stop=(k == KC - 1))
                                ticks.append(T.inc(ins, "te"))
                            mm_tick.append(ticks)
                        blk.tensor(tp)

                        def sp(se, p=p, j=j, it=it):
                            T.wait(se, "se", "dls", b_loaded)
                            if it >= 2:
                                for q_, t_ in pout_free[it - 2]:
                                    T.wait(se, "se", q_, t_)
                            ticks = []
                            for nb in range(4):
                                T.wait(se, "se", "te", mm_tick[it][nb])
                                T.sw(se, "se")
                                ticks.append(T.inc(se.activation(
                                    pouts[it % 2][:, nb * 512:(nb + 1) * 512],
                                    pmL[:, nb * 512:(nb + 1) * 512], AF.Identity,
                                    bias=bqp[:, j * NPAIR + p:j * NPAIR + p + 1]),
                                    "se"))
                            ev_tick.append(ticks)
                        blk.scalar(sp)

                        if j < 2:
                            vsq_last = []

                            def vsq(ve, j=j, it=it, nt=norm_te_j[j]):
                                T.wait(ve, "ve", "te", nt)
                                last = None
                                for nb in range(4):
                                    T.wait(ve, "ve", "se", ev_tick[it][nb])
                                    T.sw(ve, "ve")
                                    last = T.inc(ve.tensor_tensor(
                                        psqs[j][:, nb * 512:(nb + 1) * 512],
                                        pouts[it % 2][:, nb * 512:(nb + 1) * 512],
                                        pouts[it % 2][:, nb * 512:(nb + 1) * 512],
                                        ALU.mult), "ve")
                                vsq_last.append(last)
                            blk.vector(vsq)

                            def gsp(ge, p=p, j=j, it=it):
                                T.wait(ge, "ge", "se", ev_tick[it][3])
                                T.dma(ge, "ge", "dsg",
                                      out=(g_q[p] if j == 0 else g_k[p]),
                                      in_=pouts[it % 2][:])
                            blk.gpsimd(gsp)
                            pout_free.append([("ve", vsq_last[0]),
                                              ("dsg", T.cnt["dsg"])])
                            if pending_norm is not None:
                                pj, ptick = pending_norm
                                emit_norm(psqs[pj], prowD, nrm2,
                                          SC3 if pj == 0 else SC4, ("ve", ptick))
                                norm_te_j[pj] = T.cnt["te"]
                            pending_norm = (j, vsq_last[0])
                        else:
                            vtp_loc, vcp_loc = [], []
                            vgate = vspl_byp.get(p - 1)
                            for st_ in range(TT):
                                def tvi(te, st_=st_, it=it):
                                    T.wait(te, "te", "se", ev_tick[it][st_ // 4])
                                    if st_ >= 2:
                                        T.wait(te, "te", "se", vcp_loc[st_ - 2])
                                    vtp_loc.append(
                                        T.inc(te.transpose(
                                            pmTr[:, (st_ % 2) * 512:
                                                 (st_ % 2) * 512 + P],
                                            pouts[it % 2][:, st_ * P:(st_ + 1) * P],
                                            ident_f[:]), "te"))
                                blk.tensor(tvi)

                                def svi(se, st_=st_, vgate=vgate):
                                    T.wait(se, "se", "te", vtp_loc[st_])
                                    if st_ == 0 and vgate is not None:
                                        T.wait(se, "se", "dsg", vgate)
                                    T.sw(se, "se")
                                    se.copy(vstag[0][:, st_ * 65:st_ * 65 + DK],
                                            pmTr[:, (st_ % 2) * 512:
                                                 (st_ % 2) * 512 + DK])
                                    vcp_loc.append(T.inc(
                                        se.copy(vstag[1][:, st_ * 65:st_ * 65 + DK],
                                                pmTr[:, (st_ % 2) * 512 + DK:
                                                     (st_ % 2) * 512 + P]), "se"))
                                blk.scalar(svi)

                            def gspv(ge, p=p):
                                T.wait(ge, "ge", "se", vcp_loc[TT - 1])
                                if p == 0:
                                    T.wait(ge, "ge", "ve", vz0_ve)
                                T.dma(ge, "ge", "dsg", out=g_v[2 * p],
                                      in_=vstag[0][:])
                                vspl_byp[p] = T.dma(ge, "ge", "dsg",
                                                    out=g_v[2 * p + 1],
                                                    in_=vstag[1][:])
                            blk.gpsimd(gspv)
                            pout_free.append([("te", vtp_loc[TT - 1])])
                if pending_norm is not None:
                    pj, ptick = pending_norm
                    emit_norm(psqs[pj], prowD, nrm2,
                              SC3 if pj == 0 else SC4, ("ve", ptick))
                emit_negM()
                emit_negM_bcast(pmTr)

            ctx = actx.enter_context(nc.sbuf_tensor("ctxL0", [P, TT * D], F32))
            scS = actx.enter_context(contextlib.ExitStack())
            vfA = [scS.enter_context(nc.sbuf_tensor(f"vfA{h}", [P, TT * 65], F32))
                   for h in range(2)]
            vfB = [scS.enter_context(nc.sbuf_tensor(f"vfB{h}", [P, TT * 65], F32))
                   for h in range(2)]
            kb0 = scS.enter_context(nc.sbuf_tensor("kb0L0", [P, S], F32))
            kb1 = scS.enter_context(nc.sbuf_tensor("kb1L0", [P, S], F32))
            sc_snap = T.snap()

            def vload(sy_, p, _qb):
                T.dma(sy_, "sy", "dls", out=_qb[:], in_=g_q[p])
                kb = [kb0, kb1][p % 2]
                T.dma(sy_, "sy", "dls", out=kb[:], in_=g_k[p])
                for h in range(2):
                    vf = [vfA, vfB][p % 2][h]
                    T.dma(sy_, "sy", "dls", out=vf[:], in_=g_v[2 * p + h])

            emit_sc(0, ctx,
                    kb_getter=lambda p: [kb0, kb1][p % 2],
                    vf_getter=lambda p: [vfA, vfB][p % 2],
                    load_pair=vload)

            scS.close()
            emit_wo(ctx, w["wo"], has_bias=True)

    def attn_l1():
        snap0 = T.snap()
        with contextlib.ExitStack() as actx:
            kdup = actx.enter_context(nc.sbuf_tensor("kdupL1", [P, S], F32))
            vfL1 = actx.enter_context(nc.sbuf_tensor("vfL1", [P, TT * 65], F32))
            latS = actx.enter_context(contextlib.ExitStack())
            latT = latS.enter_context(nc.sbuf_tensor("latTL1", [P, 4 * S], F32))

            with contextlib.ExitStack() as s1:
                xlnT = attn_lnT(snap0, s1)
                pj_snap1 = T.snap()
                wcd = s1.enter_context(nc.sbuf_tensor("wcdL1", [P, KC * 512], F32))
                pmL = s1.enter_context(nc.psum_tensor("pmL1a", [P, S], F32))

                def syw(sy_):
                    T.wait_all(sy_, "sy", pj_snap1)
                    for k in range(KC):
                        T.dma(sy_, "sy", "dls", out=wcd[:, k * 512:(k + 1) * 512],
                                            in_=w["wcdown"][k * P:(k + 1) * P, :])
                blk.sync(syw)
                wcd_loaded = T.cnt["dls"]
                lat_se = []
                lat_snap = T.snap()
                for m in range(4):
                    def tl(te, m=m):
                        T.wait(te, "te", "dls", wcd_loaded)
                        if m == 0:
                            T.wait_all(te, "te", lat_snap)
                        if m >= 1:
                            T.wait(te, "te", "se", lat_se[m - 1])
                        for nb in range(4):
                            for k in range(KC):
                                ins = te.matmul(
                                    pmL[:, nb * 512:(nb + 1) * 512],
                                    wcd[:, k * 512 + m * P:k * 512 + (m + 1) * P],
                                    xlnT[:, k * S + nb * 512:k * S + (nb + 1) * 512],
                                    start=(k == 0), stop=(k == KC - 1))
                        T.inc(ins, "te")
                    blk.tensor(tl)

                    def sl(se, m=m):
                        T.wait(se, "se", "te", T.cnt["te"])
                        for nb in range(4):
                            T.sw(se, "se")
                            ins = T.inc(se.copy(
                                latT[:, m * S + nb * 512:m * S + (nb + 1) * 512],
                                pmL[:, nb * 512:(nb + 1) * 512]), "se")
                        lat_se.append(ins)
                    blk.scalar(sl)

            with contextlib.ExitStack() as s2:
                wqu = s2.enter_context(nc.sbuf_tensor("wquL1", [P, 4 * D], F32))
                wkv = s2.enter_context(nc.sbuf_tensor("wkvL1", [P, 512], F32))
                pout = s2.enter_context(nc.sbuf_tensor("poutL1", [P, S], F32))
                psqs1 = [s2.enter_context(nc.sbuf_tensor(f"psq{jj}L1", [P, S], F32))
                         for jj in range(2)]
                kvT = s2.enter_context(nc.sbuf_tensor("kvTL1", [P, S], F32))
                vlow = s2.enter_context(nc.sbuf_tensor("vlowL1", [P, S], F32))
                pmL = s2.enter_context(nc.psum_tensor("pmL1b", [P, S], F32))
                pmTr = s2.enter_context(nc.psum_tensor("pmTr1", [P, D], F32))
                prowD = s2.enter_context(nc.psum_tensor("prowD1", [2, D], F32))
                nrm2 = s2.enter_context(nc.sbuf_tensor("nrm2L1", [2, 512], F32))
                s2_snap = T.snap()

                def vz0(ve):
                    T.wait_all(ve, "ve", s2_snap)
                    T.sw(ve, "ve")
                    T.inc(ve.memset(stats[0:2, SC3:SC4 + 1], 0.0), "ve")
                    ap = vfL1[:].rearrange("p (s f) -> p s f", f=65)[:, :, DK:65]
                    T.sw(ve, "ve")
                    T.inc(ve.memset(ap, 1.0), "ve")
                blk.vector(vz0)

                def syw2(sy_):
                    T.wait_all(sy_, "sy", s2_snap)
                    for lk in range(4):
                        T.dma(sy_, "sy", "dls", out=wqu[:, lk * D:(lk + 1) * D],
                                            in_=w["wqup"][lk * P:(lk + 1) * P, :])
                    for lk in range(4):
                        T.dma(sy_, "sy", "dls", out=wkv[:, lk * P:(lk + 1) * P],
                                            in_=w["wkvup"][lk * P:(lk + 1) * P, :])
                blk.sync(syw2)
                wq_loaded = T.cnt["dls"]

                ev_t, spl_t, sqse_t = [], [], []
                norm_te1 = [0, 0]
                pending1 = None
                for p in range(NPAIR):
                    def tq(te, p=p):
                        T.wait(te, "te", "dls", wq_loaded)
                        if p == 0:
                            T.wait_all(te, "te", s2_snap)
                        if p >= 1:
                            T.wait(te, "te", "se", ev_t[p - 1])
                        for nb in range(4):
                            for lk in range(4):
                                ins = te.matmul(
                                    pmL[:, nb * 512:(nb + 1) * 512],
                                    wqu[:, lk * D + p * P:lk * D + (p + 1) * P],
                                    latT[:, lk * S + nb * 512:lk * S + (nb + 1) * 512],
                                    start=(lk == 0), stop=(lk == 3))
                        T.inc(ins, "te")
                    blk.tensor(tq)

                    def sq(se, p=p, nt=norm_te1[p % 2]):
                        T.wait(se, "se", "te", T.cnt["te"])
                        if p >= 1:
                            T.wait(se, "se", "dss", spl_t[p - 1])
                        for nb in range(4):
                            T.sw(se, "se")
                            ins = T.inc(se.copy(pout[:, nb * 512:(nb + 1) * 512],
                                                pmL[:, nb * 512:(nb + 1) * 512]), "se")
                        ev_t.append(ins)
                        T.wait(se, "se", "te", nt)
                        T.sw(se, "se")
                        sqse_t.append(T.inc(se.activation(psqs1[p % 2][:], pout[:],
                                                          AF.Square), "se"))
                        T.sw(se, "se")
                        spl_t.append(T.dma(se, "se", "dss", out=g_q[p], in_=pout[:]))
                    blk.scalar(sq)
                    if pending1 is not None:
                        pp, ptick = pending1
                        emit_norm(psqs1[pp % 2], prowD, nrm2, SC3, ("se", ptick))
                        norm_te1[pp % 2] = T.cnt["te"]
                    pending1 = (p, sqse_t[-1])

                def tkv(te):
                    T.wait(te, "te", "se", ev_t[NPAIR - 1])
                    T.sw(te, "te")
                    for nb in range(4):
                        for lk in range(4):
                            ins = te.matmul(
                                pmL[:, nb * 512:(nb + 1) * 512],
                                wkv[:, lk * P:(lk + 1) * P],
                                latT[:, lk * S + nb * 512:lk * S + (nb + 1) * 512],
                                start=(lk == 0), stop=(lk == 3))
                    T.inc(ins, "te")
                blk.tensor(tkv)

                if pending1 is not None:
                    pp, ptick = pending1
                    emit_norm(psqs1[pp % 2], prowD, nrm2, SC3, ("se", ptick))
                    norm_te1[pp % 2] = T.cnt["te"]

                def skv(se, nt=norm_te1[0]):
                    T.wait(se, "se", "te", T.cnt["te"])
                    T.wait(se, "se", "dss", spl_t[NPAIR - 1])
                    for nb in range(4):
                        T.sw(se, "se")
                        T.inc(se.copy(kvT[:, nb * 512:(nb + 1) * 512],
                                      pmL[:, nb * 512:(nb + 1) * 512]), "se")
                    T.wait(se, "se", "te", nt)
                    T.sw(se, "se")
                    T.inc(se.activation(psqs1[0][:], kvT[:], AF.Square), "se")
                blk.scalar(skv)
                emit_norm(psqs1[0], prowD, nrm2, SC4, ("se", T.cnt["se"]))

                def sykd(sy_):
                    T.wait(sy_, "sy", "se", T.cnt["se"])
                    T.dma(sy_, "sy", "dls", out=kdup[0:DK, :], in_=kvT[0:DK, :])
                    T.dma(sy_, "sy", "dls", out=kdup[DK:P, :], in_=kvT[0:DK, :])
                    T.dma(sy_, "sy", "dls", out=vlow[0:DK, :], in_=kvT[DK:P, :])
                blk.sync(sykd)
                kv_dma = T.cnt["dls"]

                vtp_t, vcp_t1 = [], []
                for st_ in range(TT):
                    def tv(te, st_=st_):
                        if st_ == 0:
                            T.wait(te, "te", "se", T.cnt["se"])
                            T.wait(te, "te", "dls", kv_dma)
                        if st_ >= 2:
                            T.wait(te, "te", "se", vcp_t1[st_ - 2])
                        vtp_t.append(
                            T.inc(te.transpose(
                                pmTr[:, (st_ % 2) * 512:(st_ % 2) * 512 + DK],
                                vlow[0:DK, st_ * P:(st_ + 1) * P],
                                ident_f[0:DK, 0:DK]), "te"))
                    blk.tensor(tv)

                    def sv(se, st_=st_):
                        T.wait(se, "se", "te", vtp_t[st_])
                        T.sw(se, "se")
                        vcp_t1.append(
                            T.inc(se.copy(vfL1[:, st_ * 65:st_ * 65 + DK],
                                          pmTr[:, (st_ % 2) * 512:
                                               (st_ % 2) * 512 + DK]), "se"))
                    blk.scalar(sv)

                emit_negM()
                emit_negM_bcast(pmTr)

            latS.close()
            ctx = actx.enter_context(nc.sbuf_tensor("ctxL1", [P, TT * D], F32))

            def vload1(sy_, p, _qb):
                T.dma(sy_, "sy", "dls", out=_qb[:], in_=g_q[p])

            emit_sc(1, ctx,
                    kb_getter=lambda p: kdup,
                    vf_getter=lambda p: [vfL1, vfL1],
                    load_pair=vload1)
            emit_wo(ctx, w["wcproj"], has_bias=False)

    consts_and_x()
    with contextlib.ExitStack() as pes0:
        ffn_prep(0, pes0)
    if N_SUBLAYERS >= 1:
        ffn_call(0, "f0a")
    if N_SUBLAYERS >= 2:
        attn_l0()
    if N_SUBLAYERS >= 3:
        ffn_call(0, "f0b")
    if N_SUBLAYERS >= 4:
        with contextlib.ExitStack() as pes1:
            ffn_prep(1, pes1)
        ffn_call(1, "f1a")
    if N_SUBLAYERS >= 5:
        attn_l1()
    if N_SUBLAYERS >= 6:
        ffn_call(1, "f1b")

    snapF = T.snap()

    def sfin(se):
        T.wait_all(se, "se", snapF)
        T.sw(se, "se")
        for i in range(TT):
            T.dma(se, "se", "dss", out=out_h[i * P:(i + 1) * P, :],
                               in_=xres[:, i * D:(i + 1) * D])
    blk.scalar(sfin)

    def syfin(sy_):
        T.wait(sy_, "sy", "dss", T.cnt["dss"])
    blk.sync(syfin)

    T.check()
    es.close()
    return nc


def kernel(**inputs):
    f32 = np.float32

    def c(a):
        return np.ascontiguousarray(np.asarray(a), dtype=f32)

    common = {}
    for L in (0, 1):
        common[f"w1t{L}"] = c(np.asarray(inputs[f"L{L}_ffn_w1"]).T)
        common[f"w2t{L}"] = c(np.asarray(inputs[f"L{L}_ffn_w2"]).T)
        common[f"cw{L}"] = c(np.asarray(inputs[f"L{L}_ffn_conv_w"]).reshape(
            ST, P, 3).transpose(1, 0, 2).reshape(P, ST * 3))
        common[f"cb{L}"] = c(np.asarray(inputs[f"L{L}_ffn_conv_b"]).reshape(ST, P).T)
    for nm, key in (("wq", "L0_attn_Wq"), ("wk", "L0_attn_Wk"), ("wv", "L0_attn_Wv")):
        W = np.asarray(inputs[key])
        common[nm] = c(W.reshape(NPAIR, 2, KC, P, DK).transpose(0, 3, 2, 1, 4)
                       .reshape(NPAIR, P, KC * P))
    for nm, key in (("bq", "L0_attn_bq"), ("bk", "L0_attn_bk"), ("bv", "L0_attn_bv")):
        common[nm] = c(np.asarray(inputs[key]).reshape(NPAIR, P).T)
    common["wo"] = c(inputs["L0_attn_Wo"])
    common["bo"] = c(np.asarray(inputs["L0_attn_bo"]).reshape(1, D))
    common["wcdown"] = c(inputs["L1_attn_Wc_down"])
    common["wqup"] = c(inputs["L1_attn_Wq_up"])
    common["wkvup"] = c(inputs["L1_attn_Wkv_up"])
    common["wcproj"] = c(inputs["L1_attn_Wc_proj"])

    x = np.asarray(inputs["L0_attn_embeddings"], dtype=f32)
    nc = build_program()
    in_maps = [dict(common, x=c(x[b])) for b in range(8)]
    res = run_bass_kernel_spmd(nc, in_maps, core_ids=list(range(8)))
    return np.stack([res.results[b]["out"] for b in range(8)], axis=0)
